# version 02
# baseline (speedup 1.0000x reference)
import numpy as np
import concourse.bass as bass
import concourse.mybir as mybir
from concourse.bass_utils import run_bass_kernel_spmd

F32 = mybir.dt.float32
AF = mybir.ActivationFunctionType
ALU = mybir.AluOpType
AX = mybir.AxisListType

MAGIC = 12582912.0
NEG = -1.0e30
S, D, NT = 2048, 1024, 16

ENGS = ("sync", "scalar", "vector", "tensor", "gpsimd")


class Prog:
    def __init__(self, nc):
        self.nc = nc
        self.ops = {e: [] for e in ENGS}
        self.sems = {}
        self.cnt = {}
        for s in ("L", "G", "T", "V", "Sc", "P", "C"):
            self.sems[s] = nc.alloc_semaphore("sem_" + s)
            self.cnt[s] = 0
        self.bank_w = [0] * 8
        self.bank_r = [[] for _ in range(8)]

    def op(self, eng, meth, *args, inc=None, n=1, drain=None, **kwargs):
        do_drain = (eng in ("vector", "scalar", "gpsimd")) if drain is None else drain
        if inc is not None:
            self.cnt[inc] += n
            h = self.sems[inc]
            if do_drain:
                self.ops[eng].append(("o", meth, args, kwargs, None, 0))
                self.ops[eng].append(("d", None, None, None, h, n))
            else:
                self.ops[eng].append(("i", meth, args, kwargs, h, n))
            return self.cnt[inc]
        self.ops[eng].append(("o", meth, args, kwargs, None, 0))
        if do_drain:
            self.ops[eng].append(("d", None, None, None, None, 0))
        return None

    def wait(self, eng, sem, val):
        if val <= 0:
            return
        self.ops[eng].append(("w", None, None, None, self.sems[sem], val))

    def dma(self, eng, out_ap, in_ap, sem=None):
        if sem is None:
            sem = "L" if eng == "sync" else "G"
        return self.op(eng, "dma_start", inc=sem, n=16, drain=False,
                       out=out_ap, in_=in_ap)

    def replay(self, eng, e):
        for kind, meth, args, kwargs, h, n in self.ops[eng]:
            if kind == "w":
                e.wait_ge(h, n)
            elif kind == "i":
                getattr(e, meth)(*args, **kwargs).then_inc(h, n)
            elif kind == "d":
                if h is not None:
                    e.drain().then_inc(h, n)
                else:
                    e.drain()
            else:
                getattr(e, meth)(*args, **kwargs)

    def t_take(self, banks):
        for b in banks:
            for (sem, val) in self.bank_r[b]:
                self.wait("tensor", sem, val)
            self.bank_r[b] = []

    def t_done(self, banks):
        v = self.cnt["T"]
        for b in banks:
            self.bank_w[b] = v

    def r_wait(self, eng, banks):
        self.wait(eng, "T", max(self.bank_w[b] for b in banks))

    def r_done(self, banks, sem):
        v = self.cnt[sem]
        for b in banks:
            self.bank_r[b].append((sem, v))

    def barrier(self):
        for eng, sem in (("vector", "V"), ("scalar", "Sc"), ("gpsimd", "P")):
            self.cnt[sem] += 1
            self.ops[eng].append(("d", None, None, None, self.sems[sem], 1))
        snap = {s: self.cnt[s] for s in ("T", "V", "Sc", "P", "G", "L")}
        for eng in ENGS:
            for s, v in snap.items():
                self.wait(eng, s, v)
        for b in range(8):
            self.bank_r[b] = []


class Arena:
    def __init__(self, ap, n, prog=None):
        self.ap, self.n, self.off, self.marks = ap, n, 0, []
        self.peak = 0
        self.prog = prog

    def alloc(self, n):
        n = (n + 7) & ~7
        assert self.off + n <= self.n, f"arena overflow {self.off}+{n}>{self.n}"
        o = self.off
        self.off += n
        self.peak = max(self.peak, self.off)
        return self.ap[:, o:o + n]

    def push(self):
        self.marks.append(self.off)

    def pop(self):
        self.off = self.marks.pop()
        if self.prog is not None:
            self.prog.barrier()


class Stream:
    def __init__(self, prog, arena, width, nbuf=2):
        self.P = prog
        self.bufs = [arena.alloc(width) for _ in range(nbuf)]
        self.free_at = [None] * nbuf
        self.i = 0

    def load(self, src_ap):
        k = self.i % len(self.bufs)
        self.i += 1
        if self.free_at[k] is not None:
            sem, val = self.free_at[k]
            self.P.wait("sync", sem, val)
            self.free_at[k] = None
        dst = self.bufs[k]
        if list(src_ap.shape) != list(dst.shape):
            dst = dst[:, 0:src_ap.shape[-1]] if src_ap.ndim == 2 else dst
        lv = self.P.dma("sync", dst, src_ap)
        return k, self.bufs[k], lv

    def release(self, k, sem):
        self.free_at[k] = (sem, self.P.cnt[sem])


def view(ap, w):
    return ap.rearrange("p (a b) -> p a b", b=w)


def emit_ln(P, ar, PS, C, src_rows, dst_fm=None, dst_tm=None, gb=None,
            absmax=None, eps=1e-5, width=D, dst_hbm=None):
    ar.push()
    xs = Stream(P, ar, width)
    tmp = ar.alloc(width) if dst_tm is None else None
    tmp2 = ar.alloc(width) if dst_hbm is not None else None
    tmp_free = [0, 0]
    st = ar.alloc(8)
    sm, mu, var, ssq, mx = st[:, 0:1], st[:, 1:2], st[:, 2:3], st[:, 3:4], st[:, 4:5]
    scr = ar.alloc(width)
    nck = width // 128
    for t in range(NT):
        k, xt, lv = xs.load(src_rows[t * 128:(t + 1) * 128, :])
        P.wait("scalar", "L", lv)
        P.op("scalar", "activation", scr, xt, AF.Square, accum_out=ssq, inc="Sc")
        P.wait("vector", "L", lv)
        P.wait("vector", "Sc", P.cnt["Sc"])
        P.op("vector", "reduce_sum", sm, xt, AX.X)
        P.op("vector", "tensor_scalar", mu, sm, 1.0 / width, None, ALU.mult)
        P.op("vector", "tensor_tensor", var, mu, mu, ALU.mult)
        P.op("vector", "tensor_scalar", ssq, ssq, 1.0 / width, eps, ALU.mult, ALU.add)
        P.op("vector", "tensor_sub", var, ssq, var, inc="V")
        P.wait("scalar", "V", P.cnt["V"])
        P.op("scalar", "sqrt", var, var, inc="Sc")
        P.wait("vector", "Sc", P.cnt["Sc"])
        P.op("vector", "reciprocal", var, var)
        dst_t = dst_tm[:, t, :] if dst_tm is not None else tmp
        if dst_hbm is not None:
            dst_t = tmp if t % 2 == 0 else tmp2
            P.wait("vector", "G", tmp_free[t % 2])
        P.op("vector", "tensor_scalar", dst_t, xt, mu, var, ALU.subtract, ALU.mult, inc="V")
        if gb is not None:
            P.op("vector", "tensor_mul", dst_t, dst_t, gb[0])
            P.op("vector", "tensor_add", dst_t, dst_t, gb[1], inc="V")
        if absmax is not None:
            P.op("vector", "tensor_reduce", mx, dst_t, AX.X, ALU.max, apply_absolute_value=True)
            P.op("vector", "tensor_max", absmax, absmax, mx, inc="V")
        xs.release(k, "V")
        if dst_hbm is not None:
            P.wait("gpsimd", "V", P.cnt["V"])
            tmp_free[t % 2] = P.dma("gpsimd", dst_hbm[t * 128:(t + 1) * 128, :], dst_t)
        if dst_fm is not None:
            P.t_take([6, 7])
            P.wait("tensor", "V", P.cnt["V"])
            for c in range(nck):
                P.op("tensor", "transpose", PS[:, 3072 + c * 128:3072 + (c + 1) * 128],
                     dst_t[:, c * 128:(c + 1) * 128], C["ident"],
                     inc="T" if c == nck - 1 else None)
            P.t_done([6, 7])
            P.r_wait("scalar", [6, 7])
            for half in range((nck + 3) // 4):
                w = min(4, nck - half * 4)
                P.op("scalar", "copy",
                     dst_fm[:, half * 4:half * 4 + w, t * 128:(t + 1) * 128],
                     view(PS[:, 3072 + half * 512:3072 + half * 512 + w * 128], 128), inc="Sc")
            P.r_done([6, 7], "Sc")
            if dst_tm is None:
                P.wait("vector", "Sc", P.cnt["Sc"])
    ar.pop()


def bcast_scalar(P, ar, PS, C, src11, dst):
    P.t_take([6])
    P.wait("tensor", "V", P.cnt["V"])
    P.op("tensor", "matmul", PS[:, 3072:3073], C["ones_row"], src11,
         start=True, stop=True, inc="T")
    P.t_done([6])
    P.r_wait("vector", [6])
    P.op("vector", "tensor_copy", dst, PS[:, 3072:3073], inc="V")
    P.r_done([6], "V")


def part_max(P, ar, PS, C, acc128, dst11):
    P.t_take([6])
    P.wait("tensor", "V", P.cnt["V"])
    P.op("tensor", "transpose", PS[0:1, 3072:3072 + 128], acc128, C["ident"], inc="T")
    P.t_done([6])
    P.r_wait("vector", [6])
    P.op("vector", "tensor_reduce", dst11, PS[0:1, 3072:3072 + 128], AX.X, ALU.max, inc="V")
    P.r_done([6], "V")


def emit_l0_attn(P, ar, PS, C, IN, x_in, xres):
    ar.push()
    fmA = ar.alloc(8 * 2048)
    fmAv = view(fmA, 2048)
    emit_ln(P, ar, PS, C, x_in, dst_fm=fmAv)

    qB = ar.alloc(16 * 1024)
    qBv = view(qB, 1024)
    ar.push()
    wq = Stream(P, ar, 1024)
    scr = ar.alloc(1024)
    st8 = ar.alloc(8)
    ss, rstd = st8[:, 0:1], st8[:, 1:2]
    l2s = ar.alloc(8)[:, 0:4]
    P.wait("tensor", "Sc", P.cnt["Sc"])
    for g in range(4):
        tiles = list(range(g * 4, g * 4 + 4))
        P.t_take(list(range(8)))
        for dc in range(8):
            k, wt, lv = wq.load(IN["L0_WqT"][dc * 128:(dc + 1) * 128, :])
            P.wait("tensor", "L", lv)
            for ti, t in enumerate(tiles):
                for h2 in range(2):
                    last = (ti == 3 and h2 == 1)
                    P.op("tensor", "matmul", PS[:, ti * 512 + h2 * 2048: ti * 512 + h2 * 2048 + 512],
                         fmAv[:, dc, t * 128:(t + 1) * 128], wt[:, h2 * 512:(h2 + 1) * 512],
                         start=(dc == 0), stop=(dc == 7), inc="T" if last else None)
            wq.release(k, "T")
        P.t_done(list(range(8)))
        P.r_wait("scalar", list(range(8)))
        P.r_wait("vector", list(range(8)))
        for ti, t in enumerate(tiles):
            qtile = qBv[:, t, :]
            for h2 in range(2):
                P.op("scalar", "copy", qtile[:, h2 * 512:(h2 + 1) * 512],
                     PS[:, ti * 512 + h2 * 2048: ti * 512 + h2 * 2048 + 512], inc="Sc")
            P.wait("vector", "Sc", P.cnt["Sc"])
            P.op("vector", "scalar_tensor_tensor", scr, qtile, 1.0, qtile,
                 ALU.mult, ALU.mult, accum_out=ss)
            P.op("vector", "tensor_scalar", ss, ss, 1.0 / 1024.0, 1e-6, ALU.mult, ALU.add, inc="V")
            P.wait("scalar", "V", P.cnt["V"])
            P.op("scalar", "sqrt", ss, ss, inc="Sc")
            P.wait("vector", "Sc", P.cnt["Sc"])
            P.op("vector", "reciprocal", rstd, ss)
            P.op("vector", "scalar_tensor_tensor", qtile, qtile, rstd, C["l0_qnw"],
                 ALU.mult, ALU.mult, inc="V")
            P.wait("scalar", "V", P.cnt["V"])
            P.op("scalar", "activation", scr, qtile, AF.Square, inc="Sc")
            P.wait("vector", "Sc", P.cnt["Sc"])
            P.op("vector", "tensor_reduce", l2s, view(scr, 256), AX.X, ALU.add, inc="V")
            P.wait("scalar", "V", P.cnt["V"])
            P.op("scalar", "sqrt", l2s, l2s, inc="Sc")
            P.wait("vector", "Sc", P.cnt["Sc"])
            P.op("vector", "tensor_scalar_add", l2s, l2s, 1e-5)
            P.op("vector", "reciprocal", l2s, l2s)
            P.op("vector", "tensor_tensor", view(qtile, 256), view(qtile, 256),
                 l2s.unsqueeze(2).broadcast_to([128, 4, 256]), ALU.mult, inc="V")
        P.r_done(list(range(8)), "V")
        P.r_done(list(range(8)), "Sc")
    ar.pop()

    ar.push()
    whd = Stream(P, ar, 3072, nbuf=2)
    qh = ar.alloc(256)
    gq = ar.alloc(512)
    hq = ar.alloc(512)
    xs = Stream(P, ar, 256)
    ro = ar.alloc(256)
    qh_free_T = 0
    ghq_free_T = 0
    ghq_free_V = 0
    P.wait("tensor", "Sc", P.cnt["Sc"])
    for h in range(4):
        kw, wt, lvw = whd.load(IN["L0_whead"][h])
        w0T = view(wt[:, 0:1024], 512)
        w2T = view(wt[:, 1024:2048], 512)
        w1T = view(wt[:, 2048:3072], 256)
        for t in range(NT):
            P.t_take([6])
            P.wait("tensor", "V", P.cnt["V"])
            for c in range(2):
                P.op("tensor", "transpose", PS[:, 3072 + c * 128:3072 + (c + 1) * 128],
                     qBv[:, t, h * 256 + c * 128: h * 256 + (c + 1) * 128], C["ident"],
                     inc="T" if c == 1 else None)
            P.t_done([6])
            P.r_wait("scalar", [6])
            P.wait("scalar", "T", qh_free_T)
            P.op("scalar", "copy", view(qh, 128), view(PS[:, 3072:3072 + 256], 128), inc="Sc")
            P.r_done([6], "Sc")
            P.t_take([4, 5])
            P.wait("tensor", "L", lvw)
            P.wait("tensor", "Sc", P.cnt["Sc"])
            for hidc in range(4):
                for dc in range(2):
                    P.op("tensor", "matmul", PS[:, 4 * 512 + hidc * 128: 4 * 512 + (hidc + 1) * 128],
                         w0T[:, dc, hidc * 128:(hidc + 1) * 128], qh[:, dc * 128:(dc + 1) * 128],
                         start=(dc == 0), stop=(dc == 1))
                for dc in range(2):
                    P.op("tensor", "matmul", PS[:, 5 * 512 + hidc * 128: 5 * 512 + (hidc + 1) * 128],
                         w2T[:, dc, hidc * 128:(hidc + 1) * 128], qh[:, dc * 128:(dc + 1) * 128],
                         start=(dc == 0), stop=(dc == 1), inc="T" if hidc == 3 else None)
            qh_free_T = P.cnt["T"]
            P.t_done([4, 5])
            P.r_wait("scalar", [4, 5])
            P.wait("scalar", "T", ghq_free_T)
            P.wait("scalar", "V", ghq_free_V)
            P.op("scalar", "activation", gq, PS[:, 4 * 512:4 * 512 + 512], AF.Silu, inc="Sc")
            P.op("scalar", "copy", hq, PS[:, 5 * 512:5 * 512 + 512], inc="Sc")
            P.r_done([4, 5], "Sc")
            P.wait("vector", "Sc", P.cnt["Sc"])
            P.op("vector", "tensor_mul", gq, gq, hq, inc="V")
            ghq_free_V = P.cnt["V"]
            P.t_take([7])
            P.wait("tensor", "V", P.cnt["V"])
            for hidc in range(4):
                P.op("tensor", "matmul", PS[:, 7 * 512:7 * 512 + 256],
                     gq[:, hidc * 128:(hidc + 1) * 128], w1T[:, hidc, :],
                     start=(hidc == 0), stop=(hidc == 3), inc="T" if hidc == 3 else None)
            ghq_free_T = P.cnt["T"]
            P.t_done([7])
            kx, xsl, lvx = xs.load(x_in[t * 128:(t + 1) * 128, h * 256:(h + 1) * 256])
            P.r_wait("vector", [7])
            P.wait("vector", "L", lvx)
            P.op("vector", "scalar_tensor_tensor", ro, PS[:, 7 * 512:7 * 512 + 256],
                 1.0, xsl, ALU.mult, ALU.add, inc="V")
            P.r_done([7], "V")
            xs.release(kx, "V")
            P.wait("gpsimd", "V", P.cnt["V"])
            gv = P.dma("gpsimd", xres[t * 128:(t + 1) * 128, h * 256:(h + 1) * 256], ro)
            P.wait("vector", "G", gv)
        whd.release(kw, "T")
    ar.pop()
    ar.pop()


def emit_allreduce_max(P, ar, CCIN, CCOUT, vals):
    stage = ar.alloc(8)[0:1, :]
    for i, v in enumerate(vals):
        P.op("vector", "tensor_copy", stage[:, i:i + 1], v)
    P.op("vector", "tensor_copy", stage[:, len(vals):8],
         vals[-1].broadcast_to([1, 8 - len(vals)]), inc="V")
    P.wait("gpsimd", "V", P.cnt["V"])
    gv = P.dma("gpsimd", CCIN, stage)
    P.wait("gpsimd", "G", gv)
    P.op("gpsimd", "collective_compute", "AllReduce", ALU.max,
         replica_groups=[[0, 1, 2, 3, 4, 5, 6, 7]], ins=[CCIN], outs=[CCOUT], inc="C", drain=False)
    P.wait("gpsimd", "C", P.cnt["C"])
    gv = P.dma("gpsimd", stage, CCOUT)
    P.wait("vector", "G", gv)
    P.wait("scalar", "G", gv)
    return stage


def emit_l0_ffn_wprep(P, ar, PS, C, IN, SCR):
    for (src, dst, O, I) in (
            (IN["L0_w_up"], SCR["wuT"], 3072, 1024),
            (IN["L0_w_gate"], SCR["wgT"], 3072, 1024),
            (IN["L0_w_down"], SCR["wdT"], 1024, 3072)):
        ar.push()
        ws = Stream(P, ar, I)
        nb = I // 32
        sc = ar.alloc(nb)
        rsc = ar.alloc(nb)
        out = ar.alloc(I)
        ev = [ar.alloc(512), ar.alloc(512)]
        ev_free = [None, None]
        nob, nic = O // 128, I // 128
        for ob in range(nob):
            k, wt, lv = ws.load(src[ob * 128:(ob + 1) * 128, :])
            P.wait("vector", "L", lv)
            P.wait("vector", "P", P.cnt["P"])
            P.op("vector", "tensor_reduce", sc, view(wt, 32), AX.X, ALU.max,
                 apply_absolute_value=True)
            P.op("vector", "tensor_scalar", sc, sc, 1.0 / 7.0, 1e-12, ALU.mult, ALU.max)
            P.op("vector", "reciprocal", rsc, sc, inc="V")
            P.wait("gpsimd", "V", P.cnt["V"])
            P.wait("gpsimd", "T", P.cnt["T"])
            scb = sc.unsqueeze(2).broadcast_to([128, nb, 32])
            rscb = rsc.unsqueeze(2).broadcast_to([128, nb, 32])
            P.op("gpsimd", "tensor_tensor", view(out, 32), view(wt, 32), rscb, ALU.mult)
            P.op("gpsimd", "tensor_scalar", out, out, MAGIC, -MAGIC, ALU.add, ALU.add)
            P.op("gpsimd", "tensor_tensor", view(out, 32), view(out, 32), scb, ALU.mult, inc="P")
            ws.release(k, "P")
            for grp in range(0, nic, 4):
                w4 = min(4, nic - grp)
                bank = (grp // 4) % 2
                eslot = (grp // 4) % 2
                P.t_take([bank])
                P.wait("tensor", "P", P.cnt["P"])
                for c in range(w4):
                    P.op("tensor", "transpose",
                         PS[:, bank * 512 + c * 128: bank * 512 + (c + 1) * 128],
                         out[:, (grp + c) * 128:(grp + c + 1) * 128], C["ident"],
                         inc="T" if c == w4 - 1 else None)
                P.t_done([bank])
                P.r_wait("scalar", [bank])
                if ev_free[eslot] is not None:
                    P.wait("scalar", "G", ev_free[eslot])
                P.op("scalar", "copy", view(ev[eslot][:, 0:w4 * 128], 128),
                     view(PS[:, bank * 512: bank * 512 + w4 * 128], 128), inc="Sc")
                P.r_done([bank], "Sc")
                P.wait("gpsimd", "Sc", P.cnt["Sc"])
                for c in range(w4):
                    ic = grp + c
                    gvv = P.dma("gpsimd", dst[(ob * nic + ic) * 128:(ob * nic + ic + 1) * 128, :],
                                ev[eslot][:, c * 128:(c + 1) * 128])
                ev_free[eslot] = P.cnt["G"]
        P.wait("sync", "G", P.cnt["G"])
        ar.pop()


def _qdq2(P, eng, dst, src, rs, s, sem):
    P.op(eng, "tensor_scalar", dst, src, rs, MAGIC, ALU.mult, ALU.add)
    P.op(eng, "tensor_scalar", dst, dst, -MAGIC, s, ALU.add, ALU.mult, inc=sem)


def emit_l0_ffn(P, ar, PS, C, IN, SCR, xres):
    ar.push()
    emit_l0_ffn_wprep(P, ar, PS, C, IN, SCR)

    fmA = ar.alloc(8 * 2048)
    fmAv = view(fmA, 2048)
    stats = ar.alloc(16)
    s128 = ar.alloc(16)
    amax = stats[:, 0:1]
    P.op("vector", "memset", amax, 0.0, inc="V")
    emit_ln(P, ar, PS, C, xres, dst_fm=fmAv, absmax=amax)
    part_max(P, ar, PS, C, amax, stats[0:1, 1:2])
    red = emit_allreduce_max(P, ar, SCR["cc_in0"], SCR["cc_out0"], [stats[0:1, 1:2]])
    P.op("vector", "tensor_scalar", stats[0:1, 2:3], red[:, 0:1], 1.0 / 32767.0, 1e-12, ALU.mult, ALU.max, inc="V")
    sA, rA = s128[:, 0:1], s128[:, 1:2]
    bcast_scalar(P, ar, PS, C, stats[0:1, 2:3], sA)
    P.op("vector", "reciprocal", rA, sA, inc="V")
    for c in range(8):
        _qdq2(P, "vector", fmAv[:, c, :], fmAv[:, c, :], rA, sA, "V")

    am2, am3 = stats[:, 3:4], stats[:, 4:5]
    P.op("vector", "memset", am2, 0.0)
    P.op("vector", "memset", am3, 0.0, inc="V")
    ar.push()
    wst = Stream(P, ar, 8 * 128, nbuf=3)
    ev = ar.alloc(512)
    mxt = ar.alloc(8)[:, 0:1]
    for mat, dst, am in ((SCR["wuT"], SCR["s_up"], am2), (SCR["wgT"], SCR["s_gate"], am3)):
        matv = mat.rearrange("(a p) q -> p a q", p=128)
        for fb in range(24):
            kw, wt, lvw = wst.load(matv[:, fb * 8:(fb + 1) * 8, :])
            wtv = view(wt, 128)
            for tp in range(4):
                bank = tp % 2
                P.t_take([bank])
                P.wait("tensor", "L", lvw)
                P.wait("tensor", "V", P.cnt["V"])
                for ic in range(8):
                    P.op("tensor", "matmul", PS[:, bank * 512:bank * 512 + 512],
                         wtv[:, ic, :], fmAv[:, ic, tp * 512:(tp + 1) * 512],
                         start=(ic == 0), stop=(ic == 7), inc="T" if ic == 7 else None)
                P.t_done([bank])
                P.r_wait("vector", [bank])
                P.op("vector", "tensor_copy", ev, PS[:, bank * 512:bank * 512 + 512])
                P.op("vector", "tensor_reduce", mxt, PS[:, bank * 512:bank * 512 + 512],
                     AX.X, ALU.max, apply_absolute_value=True)
                P.op("vector", "tensor_max", am, am, mxt, inc="V")
                P.r_done([bank], "V")
                P.wait("gpsimd", "V", P.cnt["V"])
                gv = P.dma("gpsimd", dst[(fb * 4 + tp) * 128:(fb * 4 + tp + 1) * 128, :], ev)
                P.wait("vector", "G", gv)
            wst.release(kw, "T")
    ar.pop()
    part_max(P, ar, PS, C, am2, stats[0:1, 5:6])
    part_max(P, ar, PS, C, am3, stats[0:1, 6:7])
    red = emit_allreduce_max(P, ar, SCR["cc_in1"], SCR["cc_out1"],
                             [stats[0:1, 5:6], stats[0:1, 6:7]])
    P.op("vector", "tensor_scalar", stats[0:1, 7:8], red[:, 0:1], 1.0 / 32767.0, 1e-12, ALU.mult, ALU.max)
    P.op("vector", "tensor_scalar", stats[0:1, 8:9], red[:, 1:2], 1.0 / 32767.0, 1e-12, ALU.mult, ALU.max, inc="V")
    sU, rU = s128[:, 2:3], s128[:, 3:4]
    sG, rG = s128[:, 4:5], s128[:, 5:6]
    bcast_scalar(P, ar, PS, C, stats[0:1, 7:8], sU)
    bcast_scalar(P, ar, PS, C, stats[0:1, 8:9], sG)
    P.op("vector", "reciprocal", rU, sU)
    P.op("vector", "reciprocal", rG, sG, inc="V")
    P.wait("gpsimd", "V", P.cnt["V"])

    P.op("vector", "memset", amax, 0.0, inc="V")
    ar.push()
    ust = Stream(P, ar, 512, nbuf=2)
    gst = Stream(P, ar, 512, nbuf=2)
    sig = ar.alloc(512)
    for i in range(96):
        ku, ut, lvu = ust.load(SCR["s_up"][i * 128:(i + 1) * 128, :])
        kg, gt, lvg = gst.load(SCR["s_gate"][i * 128:(i + 1) * 128, :])
        P.wait("gpsimd", "L", lvu)
        _qdq2(P, "gpsimd", ut, ut, rU, sU, "P")
        P.wait("vector", "L", lvg)
        P.wait("vector", "Sc", P.cnt["Sc"])
        _qdq2(P, "vector", gt, gt, rG, sG, "V")
        P.wait("scalar", "V", P.cnt["V"])
        P.op("scalar", "activation", sig, gt, AF.Sigmoid, inc="Sc")
        P.wait("vector", "Sc", P.cnt["Sc"])
        P.op("vector", "tensor_scalar", sig, sig, 65536.0, MAGIC, ALU.mult, ALU.add)
        P.op("vector", "tensor_scalar", sig, sig, -MAGIC, 65535.0, ALU.add, ALU.min)
        P.op("vector", "scalar_tensor_tensor", sig, sig, 1.0 / 65536.0, gt, ALU.mult, ALU.mult)
        P.op("vector", "tensor_reduce", mxt, sig, AX.X, ALU.max, apply_absolute_value=True)
        P.op("vector", "tensor_max", amax, amax, mxt, inc="V")
        P.wait("gpsimd", "V", P.cnt["V"])
        gv = P.dma("gpsimd", SCR["s_gate"][i * 128:(i + 1) * 128, :], sig)
        gv2 = P.dma("gpsimd", SCR["s_up"][i * 128:(i + 1) * 128, :], ut)
        ust.release(ku, "G")
        gst.release(kg, "V")
        P.wait("vector", "G", gv)
    ar.pop()
    part_max(P, ar, PS, C, amax, stats[0:1, 9:10])
    red = emit_allreduce_max(P, ar, SCR["cc_in2"], SCR["cc_out2"], [stats[0:1, 9:10]])
    P.op("vector", "tensor_scalar", stats[0:1, 10:11], red[:, 0:1], 1.0 / 32767.0, 1e-12, ALU.mult, ALU.max, inc="V")
    sT, rT = s128[:, 6:7], s128[:, 7:8]
    bcast_scalar(P, ar, PS, C, stats[0:1, 10:11], sT)
    P.op("vector", "reciprocal", rT, sT, inc="V")

    P.op("vector", "memset", amax, 0.0, inc="V")
    ar.push()
    ust = Stream(P, ar, 512, nbuf=2)
    gst = Stream(P, ar, 512, nbuf=2)
    for i in range(96):
        kg, gt, lvg = gst.load(SCR["s_gate"][i * 128:(i + 1) * 128, :])
        ku, ut, lvu = ust.load(SCR["s_up"][i * 128:(i + 1) * 128, :])
        P.wait("vector", "L", max(lvg, lvu))
        _qdq2(P, "vector", gt, gt, rT, sT, "V")
        P.op("vector", "tensor_mul", gt, gt, ut)
        P.op("vector", "tensor_reduce", mxt, gt, AX.X, ALU.max, apply_absolute_value=True)
        P.op("vector", "tensor_max", amax, amax, mxt, inc="V")
        P.wait("gpsimd", "V", P.cnt["V"])
        gv = P.dma("gpsimd", SCR["s_gate"][i * 128:(i + 1) * 128, :], gt)
        gst.release(kg, "G")
        ust.release(ku, "V")
        P.wait("vector", "G", gv)
    ar.pop()
    part_max(P, ar, PS, C, amax, stats[0:1, 11:12])
    red = emit_allreduce_max(P, ar, SCR["cc_in3"], SCR["cc_out3"], [stats[0:1, 11:12]])
    P.op("vector", "tensor_scalar", stats[0:1, 12:13], red[:, 0:1], 1.0 / 32767.0, 1e-12, ALU.mult, ALU.max, inc="V")
    sO, rO = s128[:, 8:9], s128[:, 9:10]
    bcast_scalar(P, ar, PS, C, stats[0:1, 12:13], sO)
    P.op("vector", "reciprocal", rO, sO, inc="V")

    ar.push()
    ost = Stream(P, ar, 512, nbuf=3)
    wdbuf = [ar.alloc(1024), ar.alloc(1024)]
    wd_free = [None, None]
    xs = Stream(P, ar, 1024, nbuf=2)
    rx = ar.alloc(1024)
    wdT3 = SCR["wdT"].rearrange("(h c p) q -> h c p q", c=24, p=128)
    for tp in range(4):
        P.t_take(list(range(8)))
        for ic in range(24):
            ko, ot, lvo = ost.load(SCR["s_gate"][(ic * 4 + tp) * 128:(ic * 4 + tp + 1) * 128, :])
            P.wait("vector", "L", lvo)
            _qdq2(P, "vector", ot, ot, rO, sO, "V")
            slot = ic % 2
            if wd_free[slot] is not None:
                P.wait("sync", "T", wd_free[slot])
            lvw = P.dma("sync", view(wdbuf[slot], 128),
                        wdT3[:, ic].transpose([1, 0, 2]))
            P.wait("tensor", "V", P.cnt["V"])
            P.wait("tensor", "L", lvw)
            for t4 in range(4):
                for half in range(2):
                    P.op("tensor", "matmul",
                         PS[:, (t4 * 2 + half) * 512:(t4 * 2 + half) * 512 + 512],
                         ot[:, t4 * 128:(t4 + 1) * 128],
                         wdbuf[slot][:, half * 512:(half + 1) * 512],
                         start=(ic == 0), stop=(ic == 23),
                         inc="T" if (t4 == 3 and half == 1) else None)
            ost.release(ko, "T")
            wd_free[slot] = P.cnt["T"]
        P.t_done(list(range(8)))
        P.r_wait("vector", list(range(8)))
        for t4 in range(4):
            t = tp * 4 + t4
            kx, xt, lvx = xs.load(xres[t * 128:(t + 1) * 128, :])
            P.wait("vector", "L", lvx)
            P.wait("vector", "G", P.cnt["G"])
            P.op("vector", "tensor_add", rx[:, 0:512], PS[:, t4 * 1024:t4 * 1024 + 512],
                 xt[:, 0:512])
            P.op("vector", "tensor_add", rx[:, 512:1024],
                 PS[:, t4 * 1024 + 512:t4 * 1024 + 1024], xt[:, 512:1024], inc="V")
            xs.release(kx, "V")
            P.wait("gpsimd", "V", P.cnt["V"])
            gv = P.dma("gpsimd", xres[t * 128:(t + 1) * 128, :], rx)
        P.r_done(list(range(8)), "V")
    P.wait("sync", "G", P.cnt["G"])
    ar.pop()
    ar.pop()


def emit_l1_attn(P, ar, PS, C, IN, SCR, xres):
    ar.push()
    fmA = ar.alloc(8 * 2048)
    fmAv = view(fmA, 2048)
    tmB = ar.alloc(16 * 1024)
    tmBv = view(tmB, 1024)
    emit_ln(P, ar, PS, C, xres, dst_fm=fmAv, dst_tm=tmBv)
    ar.push()
    wst = Stream(P, ar, 1024)
    bcol = ar.alloc(8)
    P.dma("sync", bcol, IN["L1_bcol"])
    lvW = P.cnt["L"]
    lws = Stream(P, ar, 2048)
    xtr = ar.alloc(8 * 128)
    xtrv = view(xtr, 128)
    Pb = ar.alloc(2048)
    PT = ar.alloc(2048)
    xs = Stream(P, ar, 1024)
    ro = ar.alloc(1024)
    st = ar.alloc(8)
    rs, nm = st[:, 0:1], st[:, 1:2]
    P.wait("vector", "L", lvW)
    P.wait("tensor", "Sc", P.cnt["Sc"])
    P.wait("tensor", "V", P.cnt["V"])
    xtr_free_T = 0
    pb_free = 0
    pt_free = 0
    ro_free = 0
    for qc in range(NT):
        kl, lwt, lvlw = lws.load(IN["L1_lw"][qc * 128:(qc + 1) * 128, :])
        P.t_take(list(range(8)))
        for dc in range(8):
            kw, wt, lvw = wst.load(IN["L1_W"][dc * 128:(dc + 1) * 128, :])
            P.wait("tensor", "L", lvw)
            for oc in range(8):
                off = oc * 512
                P.op("tensor", "matmul", PS[:, off:off + 128],
                     wt[:, oc * 128:(oc + 1) * 128],
                     fmAv[:, dc, qc * 128:(qc + 1) * 128],
                     start=(dc == 0), stop=(dc == 7), inc="T" if oc == 7 else None)
            wst.release(kw, "T")
        P.t_done(list(range(8)))
        P.r_wait("vector", list(range(8)))
        P.wait("vector", "T", xtr_free_T)
        P.op("vector", "scalar_tensor_tensor",
             view(xtr, 128),
             PS[:, 0:4096].rearrange("p (c z q) -> p c z q", z=4, q=128)[:, :, 0, :],
             1.0, bcol.unsqueeze(2).broadcast_to([128, 8, 128]),
             ALU.mult, ALU.add, inc="V")
        P.r_done(list(range(8)), "V")
        P.t_take([0, 1, 2, 3])
        P.wait("tensor", "V", P.cnt["V"])
        for kt in range(4):
            for dc in range(8):
                P.op("tensor", "matmul", PS[:, kt * 512:(kt + 1) * 512],
                     xtrv[:, dc, :], fmAv[:, dc, kt * 512:(kt + 1) * 512],
                     start=(dc == 0), stop=(dc == 7), inc="T" if dc == 7 else None)
        xtr_free_T = P.cnt["T"]
        P.t_done([0, 1, 2, 3])
        P.r_wait("vector", [0, 1, 2, 3])
        P.wait("vector", "L", lvlw)
        P.wait("vector", "T", pb_free)
        P.wait("vector", "Sc", P.cnt["Sc"])
        P.op("vector", "tensor_tensor", Pb, PS[:, 0:2048], lwt, ALU.mult, inc="V")
        P.r_done([0, 1, 2, 3], "V")
        lws.release(kl, "V")
        P.wait("scalar", "V", P.cnt["V"])
        P.op("scalar", "activation", Pb, Pb, AF.Tanh, inc="Sc")
        P.op("scalar", "activation", Pb, Pb, AF.Exp, accum_out=rs, inc="Sc")
        P.wait("vector", "Sc", P.cnt["Sc"])
        P.op("vector", "tensor_sub", Pb[:, qc * 128:(qc + 1) * 128],
             Pb[:, qc * 128:(qc + 1) * 128], C["ident"])
        P.op("vector", "tensor_scalar", rs, rs, -1.0 + 1e-7, None, ALU.add)
        P.op("vector", "reciprocal", rs, rs, inc="V")
        for grp in range(4):
            bank = 4 + grp % 2
            P.t_take([bank])
            P.wait("tensor", "V", P.cnt["V"])
            P.wait("tensor", "Sc", P.cnt["Sc"])
            for c in range(4):
                P.op("tensor", "transpose",
                     PS[:, bank * 512 + c * 128:bank * 512 + (c + 1) * 128],
                     Pb[:, (grp * 4 + c) * 128:(grp * 4 + c + 1) * 128], C["ident"],
                     inc="T" if c == 3 else None)
            pb_free = P.cnt["T"]
            P.t_done([bank])
            P.r_wait("scalar", [bank])
            P.wait("scalar", "T", pt_free)
            P.op("scalar", "copy", view(PT[:, grp * 512:(grp + 1) * 512], 128),
                 view(PS[:, bank * 512:bank * 512 + 512], 128), inc="Sc")
            P.r_done([bank], "Sc")
        P.t_take([6, 7])
        P.wait("tensor", "Sc", P.cnt["Sc"])
        for kc in range(NT):
            for half in range(2):
                P.op("tensor", "matmul", PS[:, 6 * 512 + half * 512: 6 * 512 + half * 512 + 512],
                     PT[:, kc * 128:(kc + 1) * 128], tmBv[:, kc, half * 512:(half + 1) * 512],
                     start=(kc == 0), stop=(kc == NT - 1),
                     inc="T" if (kc == NT - 1 and half == 1) else None)
        pt_free = P.cnt["T"]
        P.t_done([6, 7])
        kx, xt, lvx = xs.load(xres[qc * 128:(qc + 1) * 128, :])
        P.r_wait("vector", [6, 7])
        P.wait("vector", "L", lvx)
        P.wait("vector", "G", ro_free)
        P.op("vector", "tensor_scalar", ro, PS[:, 6 * 512:6 * 512 + 1024], rs, None, ALU.mult)
        P.op("vector", "tensor_add", ro, ro, xt, inc="V")
        P.r_done([6, 7], "V")
        xs.release(kx, "V")
        P.wait("gpsimd", "V", P.cnt["V"])
        ro_free = P.dma("gpsimd", xres[qc * 128:(qc + 1) * 128, :], ro)
    P.wait("sync", "G", P.cnt["G"])
    ar.pop()
    ar.pop()


def emit_l1_w_quant(P, ar, PS, C, src, dst, nelem, nchunks, width, sc_slot, rs_slot):
    ar.push()
    ws = Stream(P, ar, width, nbuf=3)
    acc = ar.alloc(8)
    stg = ar.alloc(64)[:, 0:width // 128]
    a0, a1 = acc[:, 0:1], acc[:, 1:2]
    P.op("vector", "memset", a0, 0.0, inc="V")
    for c in range(nchunks):
        k, wt, lv = ws.load(src[c * 128:(c + 1) * 128, :])
        P.wait("vector", "L", lv)
        P.op("vector", "tensor_reduce", stg, view(wt, 128), AX.X, ALU.add,
             apply_absolute_value=True)
        P.op("vector", "tensor_reduce", a1, stg, AX.X, ALU.add)
        P.op("vector", "tensor_add", a0, a0, a1, inc="V")
        ws.release(k, "V")
    ar.pop()
    P.t_take([6])
    P.wait("tensor", "V", P.cnt["V"])
    P.op("tensor", "matmul", PS[0:1, 3072:3073], C["ones_col"], a0, start=True, stop=True, inc="T")
    P.t_done([6])
    P.r_wait("vector", [6])
    g11 = acc[0:1, 2:3]
    P.op("vector", "tensor_scalar", g11, PS[0:1, 3072:3073], 1.0 / nelem, 1e-5,
         ALU.mult, ALU.max)
    P.op("vector", "reciprocal", g11, g11, inc="V")
    P.r_done([6], "V")
    sc = acc[:, 3:4]
    bcast_scalar(P, ar, PS, C, g11, sc)
    rsc = acc[:, 4:5]
    P.op("vector", "reciprocal", rsc, sc, inc="V")
    P.op("vector", "tensor_copy", sc_slot, sc)
    P.op("vector", "tensor_copy", rs_slot, rsc, inc="V")
    P.wait("gpsimd", "V", P.cnt["V"])
    ar.push()
    ws = Stream(P, ar, width, nbuf=3)
    for c in range(nchunks):
        k, wt, lv = ws.load(src[c * 128:(c + 1) * 128, :])
        P.wait("gpsimd", "L", lv)
        P.op("gpsimd", "tensor_scalar", wt, wt, sc_slot, MAGIC, ALU.mult, ALU.add)
        P.op("gpsimd", "tensor_scalar", wt, wt, -MAGIC, -1.0, ALU.add, ALU.max)
        P.op("gpsimd", "tensor_scalar", wt, wt, 1.0, rs_slot, ALU.min, ALU.mult, inc="P")
        gv = P.dma("gpsimd", dst[c * 128:(c + 1) * 128, :], wt)
        ws.release(k, "G")
    P.wait("sync", "G", P.cnt["G"])
    ar.pop()


def _rms_actq_t(P, ar, xt, out, scr, st, width):
    ssq, rstd, mx, qs, rq, rs2 = (st[:, 0:1], st[:, 1:2], st[:, 2:3],
                                  st[:, 3:4], st[:, 4:5], st[:, 5:6])
    P.op("scalar", "activation", scr, xt, AF.Square, accum_out=ssq, inc="Sc")
    P.wait("vector", "Sc", P.cnt["Sc"])
    P.op("vector", "tensor_scalar", ssq, ssq, 1.0 / width, 1e-5, ALU.mult, ALU.add, inc="V")
    P.wait("scalar", "V", P.cnt["V"])
    P.op("scalar", "sqrt", ssq, ssq, inc="Sc")
    P.wait("vector", "Sc", P.cnt["Sc"])
    P.op("vector", "reciprocal", rstd, ssq)
    P.op("vector", "tensor_reduce", mx, xt, AX.X, ALU.max, apply_absolute_value=True)
    P.op("vector", "tensor_tensor", mx, mx, rstd, ALU.mult)
    P.op("vector", "tensor_scalar_max", mx, mx, 1e-5)
    P.op("vector", "reciprocal", qs, mx)
    P.op("vector", "tensor_scalar", qs, qs, 127.0, None, ALU.mult)
    P.op("vector", "reciprocal", rq, qs)
    P.op("vector", "tensor_tensor", rs2, rstd, qs, ALU.mult)
    P.op("vector", "tensor_scalar", out, xt, rs2, MAGIC, ALU.mult, ALU.add)
    P.op("vector", "tensor_scalar", out, out, -MAGIC, rq, ALU.add, ALU.mult, inc="V")


def emit_l1_ffn(P, ar, PS, C, IN, SCR, xres):
    ar.push()
    sc_all = ar.alloc(8)
    emit_l1_w_quant(P, ar, PS, C, IN["L1_w1T"], SCR["wq1"], 4096 * 1024, 8, 4096,
                    sc_all[:, 0:1], sc_all[:, 1:2])
    emit_l1_w_quant(P, ar, PS, C, IN["L1_w2T"], SCR["wq2"], 4096 * 1024, 32, 1024,
                    sc_all[:, 2:3], sc_all[:, 3:4])
    fmA = ar.alloc(16 * 1024)
    fmAv = view(fmA[:, 0:8 * 2048], 2048)
    ar.push()
    tmp = ar.alloc(1024)
    xs = Stream(P, ar, 1024)
    scr = ar.alloc(1024)
    st = ar.alloc(16)
    sm, mu, var, ssq = st[:, 8:9], st[:, 9:10], st[:, 10:11], st[:, 11:12]
    lnout = ar.alloc(1024)
    for t in range(NT):
        k, xt, lv = xs.load(xres[t * 128:(t + 1) * 128, :])
        P.wait("scalar", "L", lv)
        P.wait("scalar", "V", P.cnt["V"])
        P.op("scalar", "activation", scr, xt, AF.Square, accum_out=ssq, inc="Sc")
        P.wait("vector", "L", lv)
        P.wait("vector", "Sc", P.cnt["Sc"])
        P.op("vector", "reduce_sum", sm, xt, AX.X)
        P.op("vector", "tensor_scalar", mu, sm, 1.0 / 1024, None, ALU.mult)
        P.op("vector", "tensor_tensor", var, mu, mu, ALU.mult)
        P.op("vector", "tensor_scalar", ssq, ssq, 1.0 / 1024, 1e-5, ALU.mult, ALU.add)
        P.op("vector", "tensor_sub", var, ssq, var, inc="V")
        P.wait("scalar", "V", P.cnt["V"])
        P.op("scalar", "sqrt", var, var, inc="Sc")
        P.wait("vector", "Sc", P.cnt["Sc"])
        P.op("vector", "reciprocal", var, var)
        P.op("vector", "tensor_scalar", lnout, xt, mu, var, ALU.subtract, ALU.mult, inc="V")
        xs.release(k, "V")
        P.wait("scalar", "V", P.cnt["V"])
        _rms_actq_t(P, ar, lnout, lnout, scr, st, 1024)
        P.t_take([6, 7])
        P.wait("tensor", "V", P.cnt["V"])
        for c in range(8):
            P.op("tensor", "transpose", PS[:, 3072 + c * 128:3072 + (c + 1) * 128],
                 lnout[:, c * 128:(c + 1) * 128], C["ident"], inc="T" if c == 7 else None)
        P.t_done([6, 7])
        P.r_wait("scalar", [6, 7])
        for half in range(2):
            P.op("scalar", "copy", fmAv[:, half * 4:half * 4 + 4, t * 128:(t + 1) * 128],
                 view(PS[:, 3072 + half * 512:3072 + half * 512 + 512], 128), inc="Sc")
        P.r_done([6, 7], "Sc")
        P.wait("vector", "Sc", P.cnt["Sc"])
    ar.pop()

    ar.push()
    b1rep = ar.alloc(4096)
    P.dma("sync", b1rep, IN["L1_b1_rep"])
    wst = Stream(P, ar, 512, nbuf=3)
    ev = ar.alloc(512)
    hb = ar.alloc(512)
    ev_free = 0
    hb_free = 0
    P.wait("tensor", "Sc", P.cnt["Sc"])
    P.wait("vector", "L", P.cnt["L"])
    for oc in range(8):
        for tg in range(4):
            tiles = list(range(tg * 4, tg * 4 + 4))
            bset = [tg % 2 * 4 + i for i in range(4)]
            P.t_take(bset)
            for dc in range(8):
                kw, wt, lvw = wst.load(SCR["wq1"][dc * 128:(dc + 1) * 128,
                                                  oc * 512:(oc + 1) * 512])
                P.wait("tensor", "L", lvw)
                for ti, t in enumerate(tiles):
                    P.op("tensor", "matmul", PS[:, bset[ti] * 512:bset[ti] * 512 + 512],
                         fmAv[:, dc, t * 128:(t + 1) * 128], wt,
                         start=(dc == 0), stop=(dc == 7), inc="T" if ti == 3 else None)
                wst.release(kw, "T")
            P.t_done(bset)
            P.r_wait("vector", bset)
            P.r_wait("scalar", bset)
            for ti, t in enumerate(tiles):
                P.wait("vector", "Sc", hb_free)
                P.op("vector", "tensor_tensor", hb, PS[:, bset[ti] * 512:bset[ti] * 512 + 512],
                     b1rep[:, oc * 512:(oc + 1) * 512], ALU.add, inc="V")
                P.wait("scalar", "V", P.cnt["V"])
                P.wait("scalar", "G", ev_free)
                P.op("scalar", "activation", ev, hb, AF.Gelu, inc="Sc")
                hb_free = P.cnt["Sc"]
                P.wait("gpsimd", "Sc", P.cnt["Sc"])
                ev_free = P.dma("gpsimd", SCR["s_h"][t * 128:(t + 1) * 128,
                                                     oc * 512:(oc + 1) * 512], ev)
            P.r_done(bset, "V")
            P.r_done(bset, "Sc")
    P.wait("sync", "G", P.cnt["G"])
    ar.pop()

    ar.push()
    b2rep = ar.alloc(1024)
    P.dma("sync", b2rep, IN["L1_b2_rep"])
    P.wait("vector", "L", P.cnt["L"])
    hs = Stream(P, ar, 4096, nbuf=2)
    scr = ar.alloc(4096)
    st = ar.alloc(8)
    xq2 = ar.alloc(4096)
    wst = Stream(P, ar, 1024, nbuf=3)
    xs = Stream(P, ar, 1024, nbuf=2)
    ro = ar.alloc(1024)
    ro_free = 0
    xq2_free_T = 0
    for tg in range(4):
        tiles = list(range(tg * 4, tg * 4 + 4))
        for ti, t in enumerate(tiles):
            kh, ht, lvh = hs.load(SCR["s_h"][t * 128:(t + 1) * 128, :])
            P.wait("scalar", "L", lvh)
            P.wait("scalar", "V", P.cnt["V"])
            P.wait("vector", "L", lvh)
            _rms_actq_t(P, ar, ht, xq2, scr, st, 4096)
            hs.release(kh, "V")
            for grp in range(8):
                bank = 6 + grp % 2
                P.t_take([bank])
                P.wait("tensor", "V", P.cnt["V"])
                if grp == 0:
                    P.wait("tensor", "T", xq2_free_T)
                for c in range(4):
                    P.op("tensor", "transpose",
                         PS[:, bank * 512 + c * 128:bank * 512 + (c + 1) * 128],
                         xq2[:, (grp * 4 + c) * 128:(grp * 4 + c + 1) * 128], C["ident"],
                         inc="T" if c == 3 else None)
                P.t_done([bank])
                P.r_wait("scalar", [bank])
                P.op("scalar", "copy",
                     view(fmA[:, (ti * 32 + grp * 4) * 128:(ti * 32 + grp * 4 + 4) * 128], 128),
                     view(PS[:, bank * 512:bank * 512 + 512], 128), inc="Sc")
                P.r_done([bank], "Sc")
            P.wait("vector", "Sc", P.cnt["Sc"])
        fmv = view(fmA, 128)
        bset = list(range(8))
        P.t_take(bset)
        P.wait("tensor", "Sc", P.cnt["Sc"])
        for dc in range(32):
            kw, wt, lvw = wst.load(SCR["wq2"][dc * 128:(dc + 1) * 128, :])
            P.wait("tensor", "L", lvw)
            for ti in range(4):
                for half in range(2):
                    P.op("tensor", "matmul",
                         PS[:, (ti * 2 + half) * 512:(ti * 2 + half) * 512 + 512],
                         fmv[:, ti * 32 + dc, :], wt[:, half * 512:(half + 1) * 512],
                         start=(dc == 0), stop=(dc == 31),
                         inc="T" if (ti == 3 and half == 1) else None)
            wst.release(kw, "T")
        xq2_free_T = P.cnt["T"]
        P.t_done(bset)
        P.r_wait("vector", bset)
        for ti, t in enumerate(tiles):
            kx, xt, lvx = xs.load(xres[t * 128:(t + 1) * 128, :])
            P.wait("vector", "L", lvx)
            P.wait("vector", "G", ro_free)
            P.op("vector", "tensor_add", ro[:, 0:512], PS[:, ti * 1024:ti * 1024 + 512],
                 xt[:, 0:512])
            P.op("vector", "tensor_add", ro[:, 512:1024],
                 PS[:, ti * 1024 + 512:ti * 1024 + 1024], xt[:, 512:1024], inc="V")
            xs.release(kx, "V")
            P.wait("gpsimd", "V", P.cnt["V"])
            ro_free = P.dma("gpsimd", xres[t * 128:(t + 1) * 128, :], ro)
        P.r_done(bset, "V")
    P.wait("sync", "G", P.cnt["G"])
    ar.pop()
    ar.pop()



def emit_l2_attn(P, ar, PS, C, IN, SCR, xres):
    ar.push()
    kB = ar.alloc(8 * 2048)
    kBv = view(kB, 2048)
    ar.push()
    fmA = ar.alloc(8 * 2048)
    fmAv = view(fmA, 2048)
    emit_ln(P, ar, PS, C, xres, dst_fm=fmAv)
    ar.push()
    wq = Stream(P, ar, 8 * 128, nbuf=2)
    bq = ar.alloc(16)
    P.dma("sync", bq, IN["L2_bqkv_col"])
    lvb = P.cnt["L"]
    P.wait("vector", "L", lvb)
    ev = ar.alloc(512)
    ev_free = 0
    P.wait("tensor", "Sc", P.cnt["Sc"])
    Wq3 = IN["L2_Wqkv"].rearrange("(c p) q -> p c q", p=128)
    for which in range(2):
        for oc in range(8):
            kw, wt, lvw = wq.load(Wq3[:, :, which * 1024 + oc * 128:
                                       which * 1024 + (oc + 1) * 128])
            wtv = view(wt, 128)
            P.wait("tensor", "L", lvw)
            for tp in range(4):
                bank = (oc % 2) * 2 + tp % 2
                P.t_take([bank])
                for dc in range(8):
                    P.op("tensor", "matmul", PS[:, bank * 512:bank * 512 + 512],
                         wtv[:, dc, :], fmAv[:, dc, tp * 512:(tp + 1) * 512],
                         start=(dc == 0), stop=(dc == 7), inc="T" if dc == 7 else None)
                P.t_done([bank])
                bslice = bq[:, which * 8 + oc: which * 8 + oc + 1]
                if which == 1:
                    P.r_wait("vector", [bank])
                    P.op("vector", "tensor_scalar", kBv[:, oc, tp * 512:(tp + 1) * 512],
                         PS[:, bank * 512:bank * 512 + 512], bslice, None, ALU.add, inc="V")
                    P.r_done([bank], "V")
                else:
                    P.r_wait("vector", [bank])
                    P.wait("vector", "G", ev_free)
                    P.op("vector", "tensor_scalar", ev,
                         PS[:, bank * 512:bank * 512 + 512], bslice, 0.03125,
                         ALU.add, ALU.mult, inc="V")
                    P.r_done([bank], "V")
                    P.wait("gpsimd", "V", P.cnt["V"])
                    ev_free = P.dma("gpsimd", SCR["s_q"][(oc * 4 + tp) * 128:
                                                         (oc * 4 + tp + 1) * 128, :], ev)
            wq.release(kw, "T")
    ar.pop()
    ar.push()
    wv = Stream(P, ar, 1024, nbuf=2)
    bvrep = ar.alloc(1024)
    P.dma("sync", bvrep, IN["L2_bv_rep"])
    P.wait("vector", "L", P.cnt["L"])
    ev = ar.alloc(1024)
    ev_free = 0
    for tg in range(4):
        tiles = list(range(tg * 4, tg * 4 + 4))
        bset = list(range(8))
        P.t_take(bset)
        for dc in range(8):
            kw, wt, lvw = wv.load(IN["L2_Wqkv"][dc * 128:(dc + 1) * 128, 2048:3072])
            P.wait("tensor", "L", lvw)
            for ti, t in enumerate(tiles):
                for half in range(2):
                    P.op("tensor", "matmul",
                         PS[:, (ti * 2 + half) * 512:(ti * 2 + half) * 512 + 512],
                         fmAv[:, dc, t * 128:(t + 1) * 128], wt[:, half * 512:(half + 1) * 512],
                         start=(dc == 0), stop=(dc == 7), inc="T" if (ti == 3 and half == 1) else None)
            wv.release(kw, "T")
        P.t_done(bset)
        P.r_wait("vector", bset)
        for ti, t in enumerate(tiles):
            P.wait("vector", "G", ev_free)
            P.op("vector", "scalar_tensor_tensor", ev, PS[:, ti * 1024:ti * 1024 + 1024],
                 1.0, bvrep, ALU.mult, ALU.add, inc="V")
            P.wait("gpsimd", "V", P.cnt["V"])
            ev_free = P.dma("gpsimd", SCR["s_v"][t * 128:(t + 1) * 128, :], ev)
        P.r_done(bset, "V")
    P.wait("sync", "G", P.cnt["G"])
    ar.pop()
    ar.pop()
    ar.push()
    vB = ar.alloc(16 * 1024)
    vBv = view(vB, 1024)
    for t in range(NT):
        P.dma("sync", vBv[:, t, :], SCR["s_v"][t * 128:(t + 1) * 128, :])
    lv_v = P.cnt["L"]
    wos = Stream(P, ar, 1024, nbuf=2)
    bo = ar.alloc(1024)
    P.dma("sync", bo, IN["L2_bout_rep"])
    lv_wo = P.cnt["L"]
    qs = Stream(P, ar, 8 * 128, nbuf=2)
    Pb = ar.alloc(2048)
    PT = ar.alloc(2048)
    ctx = ar.alloc(1024)
    cfm = ar.alloc(8 * 128)
    cfmv = view(cfm, 128)
    xs = Stream(P, ar, 1024)
    ro = ar.alloc(1024)
    st = ar.alloc(8)
    nm, rs = st[:, 0:1], st[:, 1:2]
    sq3 = SCR["s_q"].rearrange("(a p) q -> a p q", p=128)
    P.wait("tensor", "L", lv_v)
    P.wait("tensor", "L", lv_wo)
    pb_free = 0
    pt_free = 0
    ctx_free_T = 0
    ro_free = 0
    for qc in range(NT):
        kq, qb, lvq = qs.load(sq3[:, :, (qc % 4) * 128:(qc % 4 + 1) * 128]
                              .transpose([1, 0, 2])[:, qc // 4::4, :])
        qbv = view(qb, 128)
        P.t_take([0, 1, 2, 3])
        P.wait("tensor", "L", lvq)
        for kt in range(4):
            for dc in range(8):
                P.op("tensor", "matmul", PS[:, kt * 512:(kt + 1) * 512],
                     qbv[:, dc, :], kBv[:, dc, kt * 512:(kt + 1) * 512],
                     start=(dc == 0), stop=(dc == 7), inc="T" if dc == 7 else None)
        qs.release(kq, "T")
        P.t_done([0, 1, 2, 3])
        P.r_wait("vector", [0, 1, 2, 3])
        P.op("vector", "reduce_max", nm, PS[:, 0:2048], AX.X, negate=True, inc="V")
        P.r_wait("scalar", [0, 1, 2, 3])
        P.wait("scalar", "V", P.cnt["V"])
        P.wait("scalar", "T", pb_free)
        P.op("scalar", "activation", Pb, PS[:, 0:2048], AF.Exp, bias=nm, scale=1.0,
             accum_out=rs, inc="Sc")
        P.r_done([0, 1, 2, 3], "Sc")
        P.wait("vector", "Sc", P.cnt["Sc"])
        P.op("vector", "reciprocal", rs, rs, inc="V")
        for grp in range(4):
            bank = 4 + grp % 2
            P.t_take([bank])
            P.wait("tensor", "Sc", P.cnt["Sc"])
            for c in range(4):
                P.op("tensor", "transpose",
                     PS[:, bank * 512 + c * 128:bank * 512 + (c + 1) * 128],
                     Pb[:, (grp * 4 + c) * 128:(grp * 4 + c + 1) * 128], C["ident"],
                     inc="T" if c == 3 else None)
            pb_free = P.cnt["T"]
            P.t_done([bank])
            P.r_wait("scalar", [bank])
            P.wait("scalar", "T", pt_free)
            P.op("scalar", "copy", view(PT[:, grp * 512:(grp + 1) * 512], 128),
                 view(PS[:, bank * 512:bank * 512 + 512], 128), inc="Sc")
            P.r_done([bank], "Sc")
        P.t_take([6, 7])
        P.wait("tensor", "Sc", P.cnt["Sc"])
        for kc in range(NT):
            for half in range(2):
                P.op("tensor", "matmul",
                     PS[:, 6 * 512 + half * 512:6 * 512 + half * 512 + 512],
                     PT[:, kc * 128:(kc + 1) * 128], vBv[:, kc, half * 512:(half + 1) * 512],
                     start=(kc == 0), stop=(kc == NT - 1),
                     inc="T" if (kc == NT - 1 and half == 1) else None)
        pt_free = P.cnt["T"]
        P.t_done([6, 7])
        P.r_wait("vector", [6, 7])
        P.wait("vector", "T", ctx_free_T)
        P.op("vector", "tensor_scalar", ctx, PS[:, 6 * 512:6 * 512 + 1024], rs, None,
             ALU.mult, inc="V")
        P.r_done([6, 7], "V")
        for grp in range(2):
            bank = 4 + grp
            P.t_take([bank])
            P.wait("tensor", "V", P.cnt["V"])
            for c in range(4):
                P.op("tensor", "transpose",
                     PS[:, bank * 512 + c * 128:bank * 512 + (c + 1) * 128],
                     ctx[:, (grp * 4 + c) * 128:(grp * 4 + c + 1) * 128], C["ident"],
                     inc="T" if c == 3 else None)
            P.t_done([bank])
            P.r_wait("scalar", [bank])
            P.op("scalar", "copy", view(cfm[:, grp * 512:(grp + 1) * 512], 128),
                 view(PS[:, bank * 512:bank * 512 + 512], 128), inc="Sc")
            P.r_done([bank], "Sc")
        P.wait("sync", "V", P.cnt["V"])
        P.t_take([0, 1])
        P.wait("tensor", "Sc", P.cnt["Sc"])
        for dc in range(8):
            kw2, wo2, lvw2 = wos.load(IN["L2_Wout"][dc * 128:(dc + 1) * 128, :])
            P.wait("tensor", "L", lvw2)
            for half in range(2):
                P.op("tensor", "matmul", PS[:, half * 512:half * 512 + 512],
                     cfmv[:, dc, :], wo2[:, half * 512:(half + 1) * 512],
                     start=(dc == 0), stop=(dc == 7),
                     inc="T" if half == 1 else None)
            wos.release(kw2, "T")
        ctx_free_T = P.cnt["T"]
        P.t_done([0, 1])
        kx, xt, lvx = xs.load(xres[qc * 128:(qc + 1) * 128, :])
        P.r_wait("vector", [0, 1])
        P.wait("vector", "L", lvx)
        P.wait("vector", "G", ro_free)
        P.op("vector", "scalar_tensor_tensor", ro, PS[:, 0:1024], 1.0, bo,
             ALU.mult, ALU.add)
        P.op("vector", "tensor_add", ro, ro, xt, inc="V")
        P.r_done([0, 1], "V")
        xs.release(kx, "V")
        P.wait("gpsimd", "V", P.cnt["V"])
        ro_free = P.dma("gpsimd", xres[qc * 128:(qc + 1) * 128, :], ro)
    P.wait("sync", "G", P.cnt["G"])
    ar.pop()
    ar.pop()



def emit_l2_ffn_layer(P, ar, PS, C, IN, SCR, xres, l, xout, attn_only=False):
    NW, WIN, NH2, DH2 = 4, 512, 8, 64
    ar.push()
    qfm = ar.alloc(4 * 2048)
    qfmv = view(qfm, 2048)
    kfm = ar.alloc(4 * 2048)
    kfmv = view(kfm, 2048)
    vtm = ar.alloc(16 * 512)
    vtmv = view(vtm, 512)
    ctm = ar.alloc(16 * 512)
    ctmv = view(ctm, 512)
    mask = ar.alloc(4 * 1024)
    maskv = view(mask, 1024)
    P.dma("sync", maskv, IN["L2F_mask"])
    qsr = ar.alloc(64)
    ksr = ar.alloc(64)
    P.dma("sync", qsr, IN["L2F_qs"][l])
    lvc = P.dma("sync", ksr, IN["L2F_ks"][l])
    ar.push()
    fmt = [ar.alloc(8 * 129) for _ in range(2)]
    fmt_free = [0, 0]
    wqs = Stream(P, ar, 1536, nbuf=2)
    rope = Stream(P, ar, 4 * 64, nbuf=2)
    xs = Stream(P, ar, 1024, nbuf=2)
    qkv = [ar.alloc(1536) for _ in range(2)]
    qkv_free = [0, 0]
    scr = ar.alloc(1024)
    nrm = ar.alloc(32)
    t1 = ar.alloc(512)
    t2 = ar.alloc(512)
    P.wait("vector", "L", lvc)
    for t in range(NT):
        kx, xt, lvx = xs.load(xres[t * 128:(t + 1) * 128, :])
        sl = t % 2
        P.t_take([6, 7])
        P.wait("tensor", "L", lvx)
        for c in range(8):
            P.op("tensor", "transpose", PS[:, 3072 + c * 128:3072 + (c + 1) * 128],
                 xt[:, c * 128:(c + 1) * 128], C["ident"], inc="T" if c == 7 else None)
        P.t_done([6, 7])
        xs.release(kx, "T")
        P.r_wait("vector", [6, 7])
        P.wait("vector", "T", fmt_free[sl])
        fv = view(fmt[sl], 129)
        for c in range(8):
            P.op("vector", "tensor_copy", fv[:, c, 1:129],
                 PS[:, 3072 + c * 128:3072 + (c + 1) * 128],
                 inc="V" if c == 7 else None)
        if t == 0:
            P.op("vector", "memset", fv[:, :, 0:1], 0.0, inc="V")
        else:
            pv = view(fmt[1 - sl], 129)
            P.op("vector", "tensor_copy", fv[:, :, 0:1], pv[:, :, 128:129], inc="V")
        P.r_done([6, 7], "V")
        bset = [0, 1, 2]
        P.t_take(bset)
        P.wait("tensor", "V", P.cnt["V"])
        for dc in range(8):
            kw, wt, lvw = wqs.load(IN["L2F_wqkv"][l * 1024 + dc * 128:
                                                  l * 1024 + (dc + 1) * 128, :])
            P.wait("tensor", "L", lvw)
            off = 1 if dc < 4 else 0
            for p3 in range(3):
                P.op("tensor", "matmul", PS[:, p3 * 512:(p3 + 1) * 512],
                     fv[:, dc, off:off + 128], wt[:, p3 * 512:(p3 + 1) * 512],
                     start=(dc == 0), stop=(dc == 7), inc="T" if p3 == 2 else None)
            wqs.release(kw, "T")
        fmt_free[sl] = P.cnt["T"]
        fmt_free[1 - sl] = P.cnt["T"]
        P.t_done(bset)
        P.r_wait("scalar", bset)
        P.wait("scalar", "T", qkv_free[sl])
        qt = qkv[sl]
        P.op("scalar", "copy", view(qt, 512), view(PS[:, 0:1536], 512), inc="Sc")
        P.r_done(bset, "Sc")
        P.wait("vector", "Sc", P.cnt["Sc"])
        P.op("vector", "tensor_copy", vtmv[:, t, :], qt[:, 1024:1536], inc="V")
        kr, rt, lvr = rope.load(IN["L2F_rope"][l * 2048 + t * 128:
                                               l * 2048 + (t + 1) * 128, :])
        P.wait("scalar", "Sc", P.cnt["Sc"])
        P.op("scalar", "activation", scr, qt[:, 0:1024], AF.Square, inc="Sc")
        P.wait("vector", "Sc", P.cnt["Sc"])
        P.op("vector", "tensor_reduce", nrm[:, 0:16], view(scr, 64), AX.X, ALU.add, inc="V")
        P.wait("scalar", "V", P.cnt["V"])
        P.op("scalar", "sqrt", nrm[:, 0:16], nrm[:, 0:16], inc="Sc")
        P.wait("vector", "Sc", P.cnt["Sc"])
        P.op("vector", "tensor_scalar_max", nrm[:, 0:16], nrm[:, 0:16], 1e-12)
        P.op("vector", "reciprocal", nrm[:, 0:16], nrm[:, 0:16])
        P.wait("vector", "L", lvr)
        for qk in range(2):
            base = qk * 512
            nsl = nrm[:, qk * 8:(qk + 1) * 8]
            scv = qsr if qk == 0 else ksr
            P.op("vector", "tensor_tensor", view(qt[:, base:base + 512], 64),
                 view(qt[:, base:base + 512], 64),
                 nsl.unsqueeze(2).broadcast_to([128, 8, 64]), ALU.mult)
            P.op("vector", "tensor_tensor", view(qt[:, base:base + 512], 64),
                 view(qt[:, base:base + 512], 64),
                 scv.unsqueeze(1).broadcast_to([128, 8, 64]), ALU.mult, inc="V")
            cs = rt[:, qk * 128:qk * 128 + 64]
            ssn = rt[:, qk * 128 + 64:qk * 128 + 128]
            P.op("vector", "tensor_tensor", view(t1, 64), view(qt[:, base:base + 512], 64),
                 cs.unsqueeze(1).broadcast_to([128, 8, 64]), ALU.mult)
            P.op("vector", "tensor_tensor",
                 t2.rearrange("p (h c) -> p h c", c=32).copy() if False else view(t2, 64)[:, :, 0:32],
                 view(qt[:, base:base + 512], 64)[:, :, 32:64],
                 ssn[:, 0:32].unsqueeze(1).broadcast_to([128, 8, 32]), ALU.mult)
            P.op("vector", "tensor_tensor", view(t2, 64)[:, :, 32:64],
                 view(qt[:, base:base + 512], 64)[:, :, 0:32],
                 ssn[:, 32:64].unsqueeze(1).broadcast_to([128, 8, 32]), ALU.mult)
            P.op("vector", "tensor_add", qt[:, base:base + 512], t1, t2, inc="V")
        rope.release(kr, "V")
        P.t_take([6, 7])
        P.wait("tensor", "V", P.cnt["V"])
        for pr in range(4):
            P.op("tensor", "transpose", PS[:, 3072 + pr * 128:3072 + (pr + 1) * 128],
                 qt[:, pr * 128:(pr + 1) * 128], C["ident"])
            P.op("tensor", "transpose", PS[:, 3584 + pr * 128:3584 + (pr + 1) * 128],
                 qt[:, 512 + pr * 128:512 + (pr + 1) * 128], C["ident"],
                 inc="T" if pr == 3 else None)
        qkv_free[sl] = P.cnt["T"]
        P.t_done([6, 7])
        P.r_wait("scalar", [6, 7])
        for pr in range(4):
            P.op("scalar", "copy", qfmv[:, pr, t * 128:(t + 1) * 128],
                 PS[:, 3072 + pr * 128:3072 + (pr + 1) * 128],
                 inc="Sc" if pr == 3 else None)
            P.op("scalar", "copy", kfmv[:, pr, t * 128:(t + 1) * 128],
                 PS[:, 3584 + pr * 128:3584 + (pr + 1) * 128],
                 inc="Sc" if pr == 3 else None)
        P.r_done([6, 7], "Sc")
    ar.pop()

    ar.push()
    Pb = ar.alloc(1024)
    PT = ar.alloc(1024)
    st = ar.alloc(8)
    nm, rs = st[:, 0:1], st[:, 1:2]
    pb_free = 0
    pt_free = 0
    P.wait("tensor", "Sc", P.cnt["Sc"])
    P.wait("tensor", "V", P.cnt["V"])
    for w in range(NW):
        for h in range(NH2):
            pp, po = h // 2, (h % 2) * 64
            for a in range(4):
                qsl = qfmv[po:po + 64, pp, w * 512 + a * 128: w * 512 + (a + 1) * 128]
                kw_ = 1024 if w > 0 else 512
                kbase = (w - 1) * 512 if w > 0 else w * 512
                P.t_take([0, 1])
                for kt in range(kw_ // 512):
                    P.op("tensor", "matmul", PS[:, kt * 512:(kt + 1) * 512],
                         qsl, kfmv[po:po + 64, pp, kbase + kt * 512: kbase + (kt + 1) * 512],
                         start=True, stop=True, inc="T")
                P.t_done([0, 1])
                P.r_wait("vector", [0, 1])
                P.wait("vector", "T", pb_free)
                moff = 0 if w > 0 else 512
                P.op("vector", "tensor_tensor", Pb[:, 0:kw_], PS[:, 0:kw_],
                     maskv[:, a, moff:moff + kw_], ALU.add, inc="V")
                P.r_done([0, 1], "V")
                P.op("vector", "reduce_max", nm, Pb[:, 0:kw_], AX.X, negate=True, inc="V")
                P.wait("scalar", "V", P.cnt["V"])
                P.op("scalar", "activation", Pb[:, 0:kw_], Pb[:, 0:kw_], AF.Exp,
                     bias=nm, scale=1.0, accum_out=rs, inc="Sc")
                P.wait("vector", "Sc", P.cnt["Sc"])
                P.op("vector", "reciprocal", rs, rs, inc="V")
                for grp in range(kw_ // 512):
                    bank = 2 + grp
                    P.t_take([bank])
                    P.wait("tensor", "Sc", P.cnt["Sc"])
                    P.wait("tensor", "V", P.cnt["V"])
                    for c in range(4):
                        P.op("tensor", "transpose",
                             PS[:, bank * 512 + c * 128:bank * 512 + (c + 1) * 128],
                             Pb[:, (grp * 4 + c) * 128:(grp * 4 + c + 1) * 128],
                             C["ident"], inc="T" if c == 3 else None)
                    pb_free = P.cnt["T"]
                    P.t_done([bank])
                    P.r_wait("scalar", [bank])
                    P.wait("scalar", "T", pt_free)
                    P.op("scalar", "copy", view(PT[:, grp * 512:(grp + 1) * 512], 128),
                         view(PS[:, bank * 512:bank * 512 + 512], 128), inc="Sc")
                    P.r_done([bank], "Sc")
                P.t_take([4])
                P.wait("tensor", "Sc", P.cnt["Sc"])
                nk = kw_ // 128
                kt0 = kbase // 128
                for kc in range(nk):
                    P.op("tensor", "matmul", PS[:, 4 * 512:4 * 512 + 64],
                         PT[:, kc * 128:(kc + 1) * 128],
                         vtmv[:, kt0 + kc, h * 64:(h + 1) * 64],
                         start=(kc == 0), stop=(kc == nk - 1),
                         inc="T" if kc == nk - 1 else None)
                pt_free = P.cnt["T"]
                P.t_done([4])
                P.r_wait("vector", [4])
                P.op("vector", "tensor_scalar",
                     ctmv[:, w * 4 + a, h * 64:(h + 1) * 64],
                     PS[:, 4 * 512:4 * 512 + 64], rs, None, ALU.mult, inc="V")
                P.r_done([4], "V")
    ar.pop()

    ar.push()
    wos = Stream(P, ar, 1024, nbuf=2)
    cfm = ar.alloc(4 * 128)
    cfmv = view(cfm, 128)
    xs = Stream(P, ar, 1024, nbuf=2)
    ro = ar.alloc(1024)
    ro_free = 0
    cfm_free = 0
    for t in range(NT):
        P.t_take([6])
        P.wait("tensor", "V", P.cnt["V"])
        for c in range(4):
            P.op("tensor", "transpose", PS[:, 3072 + c * 128:3072 + (c + 1) * 128],
                 ctmv[:, t, c * 128:(c + 1) * 128], C["ident"], inc="T" if c == 3 else None)
        P.t_done([6])
        P.r_wait("scalar", [6])
        P.wait("scalar", "T", cfm_free)
        P.op("scalar", "copy", view(cfm, 128), view(PS[:, 3072:3072 + 512], 128), inc="Sc")
        P.r_done([6], "Sc")
        P.t_take([0, 1])
        P.wait("tensor", "Sc", P.cnt["Sc"])
        for dc in range(4):
            kw, wt, lvw = wos.load(IN["L2F_wo"][l * 512 + dc * 128:
                                                l * 512 + (dc + 1) * 128, :])
            P.wait("tensor", "L", lvw)
            for half in range(2):
                P.op("tensor", "matmul", PS[:, half * 512:half * 512 + 512],
                     cfmv[:, dc, :], wt[:, half * 512:(half + 1) * 512],
                     start=(dc == 0), stop=(dc == 3), inc="T" if half == 1 else None)
            wos.release(kw, "T")
        cfm_free = P.cnt["T"]
        P.t_done([0, 1])
        kx, xt, lvx = xs.load(xres[t * 128:(t + 1) * 128, :])
        P.r_wait("vector", [0, 1])
        P.wait("vector", "L", lvx)
        P.wait("vector", "G", ro_free)
        P.op("vector", "tensor_add", ro[:, 0:512], PS[:, 0:512], xt[:, 0:512])
        P.op("vector", "tensor_add", ro[:, 512:1024], PS[:, 512:1024], xt[:, 512:1024],
             inc="V")
        P.r_done([0, 1], "V")
        xs.release(kx, "V")
        P.wait("gpsimd", "V", P.cnt["V"])
        ro_free = P.dma("gpsimd", xres[t * 128:(t + 1) * 128, :], ro)
    P.wait("sync", "G", P.cnt["G"])
    ar.pop()
    ar.pop()
    if attn_only:
        return

    ar.push()
    grep_ = ar.alloc(1024)
    brep_ = ar.alloc(1024)
    P.dma("sync", grep_, IN["L2F_lng"][l * 128:(l + 1) * 128, :])
    lvgb = P.dma("sync", brep_, IN["L2F_lnb"][l * 128:(l + 1) * 128, :])
    b1rep = ar.alloc(5464)[:, 0:5460]
    lvb1 = P.dma("sync", b1rep, IN["L2F_b1_rep"][l * 128:(l + 1) * 128, :])
    P.wait("vector", "L", lvb1)
    hln = [ar.alloc(8 * 128) for _ in range(4)]
    tbuf = [ar.alloc(2816) for _ in range(4)]
    tfm = [ar.alloc(22 * 128) for _ in range(4)]
    w1s = Stream(P, ar, 512, nbuf=3)
    w2s = Stream(P, ar, 1024, nbuf=3)
    xs2 = Stream(P, ar, 1024, nbuf=2)
    hbuf = [ar.alloc(1024) for _ in range(2)]
    hbuf_free = [0, 0]
    gel = ar.alloc(512)
    gb2 = ar.alloc(512)
    ab2 = ar.alloc(512)
    scr2 = ar.alloc(1024)
    st2 = ar.alloc(8)
    sm, mu, var, ssq = st2[:, 0:1], st2[:, 1:2], st2[:, 2:3], st2[:, 3:4]
    ro2 = ar.alloc(1024)
    ro2_free = 0
    for q in range(4):
        P.op("vector", "memset", tbuf[q][:, 2730:2816], 0.0)
        P.op("vector", "memset", tbuf[q][:, 2730:2731], 1.0, inc="V")
    tfm_free = 0
    hln_free = 0
    pieces = [(0, 512), (512, 512), (1024, 512), (1536, 512), (2048, 512), (2560, 170)]
    for tg in range(4):
        tiles = list(range(tg * 4, tg * 4 + 4))
        for ti, t in enumerate(tiles):
            hsl = t % 2
            ht = hbuf[hsl]
            P.wait("sync", "T", hbuf_free[hsl])
            P.dma("sync", ht[:, 0:512], xres[t * 128:(t + 1) * 128, 0:512])
            if t == 0:
                lvh2 = P.dma("sync", ht[1:128, 512:1024], xres[0:127, 512:1024])
                P.wait("vector", "L", lvh2)
                P.op("vector", "memset", ht[0:1, 512:1024], 0.0, inc="V")
            else:
                lvh2 = P.dma("sync", ht[:, 512:1024],
                             xres[t * 128 - 1:(t + 1) * 128 - 1, 512:1024])
            P.wait("scalar", "L", lvh2)
            P.wait("scalar", "V", P.cnt["V"])
            P.op("scalar", "activation", scr2, ht, AF.Square, accum_out=ssq, inc="Sc")
            P.wait("vector", "L", lvh2)
            P.wait("vector", "Sc", P.cnt["Sc"])
            P.op("vector", "reduce_sum", sm, ht, AX.X)
            P.op("vector", "tensor_scalar", mu, sm, 1.0 / 1024, None, ALU.mult)
            P.op("vector", "tensor_tensor", var, mu, mu, ALU.mult)
            P.op("vector", "tensor_scalar", ssq, ssq, 1.0 / 1024, 1e-5, ALU.mult, ALU.add)
            P.op("vector", "tensor_sub", var, ssq, var, inc="V")
            P.wait("scalar", "V", P.cnt["V"])
            P.op("scalar", "sqrt", var, var, inc="Sc")
            P.wait("vector", "Sc", P.cnt["Sc"])
            P.op("vector", "reciprocal", var, var)
            P.op("vector", "tensor_scalar", ht, ht, mu, var, ALU.subtract, ALU.mult)
            P.op("vector", "tensor_mul", ht, ht, grep_)
            P.op("vector", "tensor_add", ht, ht, brep_, inc="V")
            P.t_take([6, 7])
            P.wait("tensor", "V", P.cnt["V"])
            for c in range(8):
                P.op("tensor", "transpose", PS[:, 3072 + c * 128:3072 + (c + 1) * 128],
                     ht[:, c * 128:(c + 1) * 128], C["ident"], inc="T" if c == 7 else None)
            hbuf_free[hsl] = P.cnt["T"]
            P.t_done([6, 7])
            P.r_wait("scalar", [6, 7])
            P.wait("scalar", "T", hln_free)
            P.op("scalar", "copy", view(hln[ti], 128),
                 view(PS[:, 3072:3072 + 1024], 128), inc="Sc")
            P.r_done([6, 7], "Sc")
            P.wait("sync", "V", P.cnt["V"])
        P.wait("tensor", "Sc", P.cnt["Sc"])
        for ip, (poff, pw) in enumerate(pieces):
            bset = list(range(8))
            P.t_take(bset)
            for dc in range(8):
                kwa, wta, lva = w1s.load(IN["L2F_w1"][l * 1024 + dc * 128:
                                                      l * 1024 + (dc + 1) * 128,
                                                      poff:poff + pw])
                kwg, wtg, lvg = w1s.load(IN["L2F_w1"][l * 1024 + dc * 128:
                                                      l * 1024 + (dc + 1) * 128,
                                                      2730 + poff:2730 + poff + pw])
                P.wait("tensor", "L", lvg)
                for ti in range(4):
                    P.op("tensor", "matmul", PS[:, (ti * 2) * 512:(ti * 2) * 512 + pw],
                         view(hln[ti], 128)[:, dc, :], wta[:, 0:pw],
                         start=(dc == 0), stop=(dc == 7))
                    P.op("tensor", "matmul",
                         PS[:, (ti * 2 + 1) * 512:(ti * 2 + 1) * 512 + pw],
                         view(hln[ti], 128)[:, dc, :], wtg[:, 0:pw],
                         start=(dc == 0), stop=(dc == 7), inc="T" if ti == 3 else None)
                w1s.release(kwa, "T")
                w1s.release(kwg, "T")
            P.t_done(bset)
            P.r_wait("scalar", bset)
            P.r_wait("vector", bset)
            for ti in range(4):
                P.wait("vector", "Sc", P.cnt["Sc"])
                P.op("vector", "tensor_tensor", gb2[:, 0:pw],
                     PS[:, (ti * 2 + 1) * 512:(ti * 2 + 1) * 512 + pw],
                     b1rep[:, 2730 + poff:2730 + poff + pw], ALU.add, inc="V")
                P.wait("scalar", "V", P.cnt["V"])
                P.op("scalar", "activation", gel[:, 0:pw], gb2[:, 0:pw], AF.Gelu, inc="Sc")
                P.wait("vector", "Sc", P.cnt["Sc"])
                P.wait("vector", "T", tfm_free)
                P.op("vector", "tensor_tensor", ab2[:, 0:pw],
                     PS[:, (ti * 2) * 512:(ti * 2) * 512 + pw],
                     b1rep[:, poff:poff + pw], ALU.add)
                P.op("vector", "tensor_tensor", tbuf[ti][:, poff:poff + pw],
                     ab2[:, 0:pw], gel[:, 0:pw], ALU.mult, inc="V")
            P.r_done(bset, "V")
            P.r_done(bset, "Sc")
        hln_free = P.cnt["T"]
        for ti in range(4):
            for grp in range(6):
                bank = 6 + grp % 2
                w4 = 4 if grp < 5 else 2
                P.t_take([bank])
                P.wait("tensor", "V", P.cnt["V"])
                for c in range(w4):
                    P.op("tensor", "transpose",
                         PS[:, bank * 512 + c * 128:bank * 512 + (c + 1) * 128],
                         tbuf[ti][:, (grp * 4 + c) * 128:(grp * 4 + c + 1) * 128],
                         C["ident"], inc="T" if c == w4 - 1 else None)
                P.t_done([bank])
                P.r_wait("scalar", [bank])
                P.wait("scalar", "T", tfm_free)
                P.op("scalar", "copy",
                     view(tfm[ti], 128)[:, grp * 4:grp * 4 + w4, :],
                     view(PS[:, bank * 512:bank * 512 + w4 * 128], 128),
                     inc="Sc")
                P.r_done([bank], "Sc")
        bset = list(range(8))
        P.t_take(bset)
        P.wait("tensor", "Sc", P.cnt["Sc"])
        P.wait("tensor", "V", P.cnt["V"])
        for dc in range(22):
            kw2, wt2, lv2 = w2s.load(IN["L2F_w2pad"][l * 2816 + dc * 128:
                                                     l * 2816 + (dc + 1) * 128, :])
            P.wait("tensor", "L", lv2)
            for ti in range(4):
                for half in range(2):
                    P.op("tensor", "matmul",
                         PS[:, (ti * 2 + half) * 512:(ti * 2 + half) * 512 + 512],
                         view(tfm[ti], 128)[:, dc, :], wt2[:, half * 512:(half + 1) * 512],
                         start=(dc == 0), stop=(dc == 21),
                         inc="T" if (ti == 3 and half == 1) else None)
            w2s.release(kw2, "T")
        tfm_free = P.cnt["T"]
        P.t_done(bset)
        P.r_wait("vector", bset)
        for ti, t in enumerate(tiles):
            kx, xt, lvx = xs2.load(xres[t * 128:(t + 1) * 128, :])
            P.wait("vector", "L", lvx)
            P.wait("vector", "G", ro2_free)
            P.op("vector", "tensor_add", ro2[:, 0:512], PS[:, ti * 1024:ti * 1024 + 512],
                 xt[:, 0:512])
            P.op("vector", "tensor_add", ro2[:, 512:1024],
                 PS[:, ti * 1024 + 512:ti * 1024 + 1024], xt[:, 512:1024], inc="V")
            xs2.release(kx, "V")
            P.wait("gpsimd", "V", P.cnt["V"])
            ro2_free = P.dma("gpsimd", xout[t * 128:(t + 1) * 128, :], ro2)
        P.r_done(bset, "V")
    P.wait("sync", "G", P.cnt["G"])
    ar.pop()


def emit_l2_ffn(P, ar, PS, C, IN, SCR, xres):
    emit_ln(P, ar, PS, C, xres, dst_hbm=SCR["s_x2"])
    P.wait("sync", "G", P.cnt["G"])
    emit_l2_ffn_layer(P, ar, PS, C, IN, SCR, SCR["s_x2"], 0, SCR["s_x3"])
    emit_l2_ffn_layer(P, ar, PS, C, IN, SCR, SCR["s_x3"], 1, SCR["s_x2"])
    ar.push()
    xs = Stream(P, ar, 1024, nbuf=2)
    ys = Stream(P, ar, 1024, nbuf=2)
    ro = ar.alloc(1024)
    ro_free = 0
    for t in range(NT):
        kx, xt, lvx = xs.load(xres[t * 128:(t + 1) * 128, :])
        ky, yt, lvy = ys.load(SCR["s_x2"][t * 128:(t + 1) * 128, :])
        P.wait("vector", "L", max(lvx, lvy))
        P.wait("vector", "G", ro_free)
        P.op("vector", "tensor_add", ro, xt, yt, inc="V")
        xs.release(kx, "V")
        ys.release(ky, "V")
        P.wait("gpsimd", "V", P.cnt["V"])
        ro_free = P.dma("gpsimd", xres[t * 128:(t + 1) * 128, :], ro)
    P.wait("sync", "G", P.cnt["G"])
    ar.pop()



def emit_l3_attn(P, ar, PS, C, IN, SCR, xres):
    ar.push()
    fmA = ar.alloc(8 * 2048)
    fmAv = view(fmA, 2048)
    emit_ln(P, ar, PS, C, xres, dst_fm=fmAv)
    ar.push()
    wst = Stream(P, ar, 1024, nbuf=3)
    brep = ar.alloc(3 * 1024)
    P.dma("sync", brep, IN["L3_bqkv_rep"])
    bq8 = ar.alloc(1024)
    P.wait("vector", "L", P.cnt["L"])
    P.op("vector", "tensor_scalar", bq8, brep[:, 0:1024], 0.125, None, ALU.mult, inc="V")
    ropes = Stream(P, ar, 64, nbuf=2)
    ev = [ar.alloc(1024), ar.alloc(1024)]
    ev_free = [0, 0]
    t1 = ar.alloc(1024)
    t2 = ar.alloc(1024)
    P.wait("tensor", "L", P.cnt["L"])
    P.wait("tensor", "Sc", P.cnt["Sc"])
    for which in range(3):
        src = IN["L3_WqT"] if which == 0 else (IN["L3_WkT"] if which == 1 else IN["L3_WvT"])
        for tg in range(4):
            tiles = list(range(tg * 4, tg * 4 + 4))
            bset = list(range(8))
            P.t_take(bset)
            for dc in range(8):
                kw, wt, lvw = wst.load(src[dc * 128:(dc + 1) * 128, :])
                P.wait("tensor", "L", lvw)
                for ti, t in enumerate(tiles):
                    for half in range(2):
                        P.op("tensor", "matmul",
                             PS[:, (ti * 2 + half) * 512:(ti * 2 + half) * 512 + 512],
                             fmAv[:, dc, t * 128:(t + 1) * 128],
                             wt[:, half * 512:(half + 1) * 512],
                             start=(dc == 0), stop=(dc == 7),
                             inc="T" if (ti == 3 and half == 1) else None)
                wst.release(kw, "T")
            P.t_done(bset)
            P.r_wait("vector", bset)
            for ti, t in enumerate(tiles):
                sl = ti % 2
                P.wait("vector", "G", ev_free[sl])
                qsc = 0.125 if which == 0 else 1.0
                badd = bq8 if which == 0 else brep[:, which * 1024:(which + 1) * 1024]
                P.op("vector", "scalar_tensor_tensor", ev[sl],
                     PS[:, ti * 1024:ti * 1024 + 1024], qsc, badd,
                     ALU.mult, ALU.add, inc="V")
                if which < 2:
                    kr, rt, lvr = ropes.load(IN["L3_rope"][t * 128:(t + 1) * 128, :])
                    P.wait("vector", "L", lvr)
                    cs = rt[:, 0:32]
                    sn = rt[:, 32:64]
                    qv = view(ev[sl], 64)
                    P.op("vector", "tensor_tensor", view(t1, 32),
                         view(ev[sl], 32),
                         cs.unsqueeze(1).broadcast_to([128, 32, 32]), ALU.mult)
                    P.op("vector", "tensor_tensor", view(t2, 64)[:, :, 0:32],
                         qv[:, :, 32:64], sn.unsqueeze(1).broadcast_to([128, 16, 32]),
                         ALU.mult)
                    P.op("vector", "tensor_scalar", view(t2, 64)[:, :, 0:32],
                         view(t2, 64)[:, :, 0:32], -1.0, None, ALU.mult)
                    P.op("vector", "tensor_tensor", view(t2, 64)[:, :, 32:64],
                         qv[:, :, 0:32], sn.unsqueeze(1).broadcast_to([128, 16, 32]),
                         ALU.mult)
                    P.op("vector", "tensor_add", ev[sl], t1, t2, inc="V")
                    ropes.release(kr, "V")
                P.wait("gpsimd", "V", P.cnt["V"])
                dst = (SCR["s_qh"], SCR["s_kh"], SCR["s_vh"])[which]
                for hd in range(16):
                    gvv = P.dma("gpsimd", dst[(hd * 16 + t) * 128:(hd * 16 + t + 1) * 128, :],
                                ev[sl][:, hd * 64:(hd + 1) * 64])
                ev_free[sl] = P.cnt["G"]
            P.r_done(bset, "V")
    P.wait("sync", "G", P.cnt["G"])
    ar.pop()
    ar.pop()
    ar.push()
    ctm = ar.alloc(16 * 1024)
    ctmv = view(ctm, 1024)
    ar.push()
    qh = ar.alloc(16 * 64)
    kh = ar.alloc(16 * 64)
    vh = ar.alloc(16 * 64)
    qfm = ar.alloc(2048)
    kfm = ar.alloc(2048)
    Pb = ar.alloc(2048)
    PT = ar.alloc(2048)
    st = ar.alloc(8)
    nm, nm2, rs, rs2 = st[:, 0:1], st[:, 1:2], st[:, 2:3], st[:, 3:4]
    dmask = ar.alloc(128)
    lvm = P.dma("sync", dmask, IN["L3_dmask"])
    P.wait("vector", "L", lvm)
    qkv_free_T = 0
    pb_free = 0
    pt_free = 0
    for hd in range(16):
        P.wait("sync", "T", qkv_free_T)
        P.dma("sync", view(qh, 64), SCR["s_qh"].rearrange("(a p) q -> p a q", p=128)
              [:, hd * 16:(hd + 1) * 16, :])
        P.dma("sync", view(kh, 64), SCR["s_kh"].rearrange("(a p) q -> p a q", p=128)
              [:, hd * 16:(hd + 1) * 16, :])
        lvq = P.dma("sync", view(vh, 64), SCR["s_vh"].rearrange("(a p) q -> p a q", p=128)
                    [:, hd * 16:(hd + 1) * 16, :])
        for grp in range(4):
            P.t_take([4, 5])
            P.wait("tensor", "L", lvq)
            for c in range(4):
                P.op("tensor", "transpose",
                     PS[0:64, 4 * 512 + c * 128:4 * 512 + (c + 1) * 128],
                     qh[:, (grp * 4 + c) * 64:(grp * 4 + c + 1) * 64], C["ident"])
                P.op("tensor", "transpose",
                     PS[0:64, 5 * 512 + c * 128:5 * 512 + (c + 1) * 128],
                     kh[:, (grp * 4 + c) * 64:(grp * 4 + c + 1) * 64], C["ident"],
                     inc="T" if c == 3 else None)
            P.t_done([4, 5])
            P.r_wait("scalar", [4, 5])
            P.wait("scalar", "T", qkv_free_T)
            P.op("scalar", "copy", view(qfm[0:64, grp * 512:(grp + 1) * 512], 128),
                 view(PS[0:64, 4 * 512:4 * 512 + 512], 128), inc="Sc")
            P.op("scalar", "copy", view(kfm[0:64, grp * 512:(grp + 1) * 512], 128),
                 view(PS[0:64, 5 * 512:5 * 512 + 512], 128), inc="Sc")
            P.r_done([4, 5], "Sc")
        P.wait("tensor", "Sc", P.cnt["Sc"])
        for qc in range(NT):
            kw_ = (qc + 1) * 128
            nkt = (kw_ + 511) // 512
            P.t_take(list(range(nkt)))
            for kt in range(nkt):
                w = min(512, kw_ - kt * 512)
                P.op("tensor", "matmul", PS[:, kt * 512:kt * 512 + w],
                     qfm[0:64, qc * 128:(qc + 1) * 128], kfm[0:64, kt * 512:kt * 512 + w],
                     start=True, stop=True, inc="T")
            P.t_done(list(range(nkt)))
            P.r_wait("vector", list(range(nkt)))
            P.wait("vector", "T", pb_free)
            P.op("vector", "tensor_tensor", Pb[:, qc * 128:kw_],
                 PS[:, qc * 128:kw_], dmask, ALU.add, inc="V")
            if qc > 0:
                P.op("vector", "reduce_max", nm, PS[:, 0:qc * 128], AX.X, negate=True)
                P.op("vector", "reduce_max", nm2, Pb[:, qc * 128:kw_], AX.X, negate=True)
                P.op("vector", "tensor_tensor", nm, nm, nm2, ALU.min, inc="V")
            else:
                P.op("vector", "reduce_max", nm, Pb[:, 0:128], AX.X, negate=True, inc="V")
            P.wait("scalar", "V", P.cnt["V"])
            P.wait("scalar", "T", pb_free)
            if qc > 0:
                P.op("scalar", "activation", Pb[:, 0:qc * 128], PS[:, 0:qc * 128],
                     AF.Exp, bias=nm, scale=1.0, accum_out=rs, inc="Sc")
            P.op("scalar", "activation", Pb[:, qc * 128:kw_], Pb[:, qc * 128:kw_],
                 AF.Exp, bias=nm, scale=1.0, accum_out=rs2, inc="Sc")
            P.r_done(list(range(nkt)), "Sc")
            P.r_done(list(range(nkt)), "V")
            P.wait("vector", "Sc", P.cnt["Sc"])
            if qc > 0:
                P.op("vector", "tensor_add", rs, rs, rs2)
            else:
                P.op("vector", "tensor_copy", rs, rs2)
            P.op("vector", "reciprocal", rs, rs, inc="V")
            for grp in range(qc // 4 + 1):
                w4 = min(4, qc + 1 - grp * 4)
                bank = 4 + grp % 2
                P.t_take([bank])
                P.wait("tensor", "Sc", P.cnt["Sc"])
                P.wait("tensor", "V", P.cnt["V"])
                for c in range(w4):
                    P.op("tensor", "transpose",
                         PS[:, bank * 512 + c * 128:bank * 512 + (c + 1) * 128],
                         Pb[:, (grp * 4 + c) * 128:(grp * 4 + c + 1) * 128], C["ident"],
                         inc="T" if c == w4 - 1 else None)
                pb_free = P.cnt["T"]
                P.t_done([bank])
                P.r_wait("scalar", [bank])
                P.wait("scalar", "T", pt_free)
                P.op("scalar", "copy", view(PT[:, grp * 512:grp * 512 + w4 * 128], 128),
                     view(PS[:, bank * 512:bank * 512 + w4 * 128], 128), inc="Sc")
                P.r_done([bank], "Sc")
            P.t_take([6])
            P.wait("tensor", "Sc", P.cnt["Sc"])
            for kc in range(qc + 1):
                P.op("tensor", "matmul", PS[:, 6 * 512:6 * 512 + 64],
                     PT[:, kc * 128:(kc + 1) * 128], vh[:, kc * 64:(kc + 1) * 64],
                     start=(kc == 0), stop=(kc == qc), inc="T" if kc == qc else None)
            pt_free = P.cnt["T"]
            P.t_done([6])
            P.r_wait("vector", [6])
            P.op("vector", "tensor_scalar", ctmv[:, qc, hd * 64:(hd + 1) * 64],
                 PS[:, 6 * 512:6 * 512 + 64], rs, None, ALU.mult, inc="V")
            P.r_done([6], "V")
        qkv_free_T = P.cnt["T"]
    ar.pop()
    ar.push()
    WoB = ar.alloc(8 * 1024)
    WoBv = view(WoB, 1024)
    for dc in range(8):
        P.dma("sync", WoBv[:, dc, :], IN["L3_WoT"][dc * 128:(dc + 1) * 128, :])
    borep = ar.alloc(1024)
    P.dma("sync", borep, IN["L3_bo_rep"])
    cfm = ar.alloc(8 * 128)
    cfmv = view(cfm, 128)
    xs = Stream(P, ar, 1024, nbuf=2)
    ro = ar.alloc(1024)
    ro_free = 0
    cfm_free = 0
    P.wait("tensor", "L", P.cnt["L"])
    for t in range(NT):
        P.t_take([4, 5])
        P.wait("tensor", "V", P.cnt["V"])
        for c in range(8):
            bank = 4 + c // 4
            P.op("tensor", "transpose",
                 PS[:, bank * 512 + (c % 4) * 128:bank * 512 + (c % 4 + 1) * 128],
                 ctmv[:, t, c * 128:(c + 1) * 128], C["ident"], inc="T" if c == 7 else None)
        P.t_done([4, 5])
        P.r_wait("scalar", [4, 5])
        P.wait("scalar", "T", cfm_free)
        P.op("scalar", "copy", view(cfm[:, 0:512], 128),
             view(PS[:, 4 * 512:4 * 512 + 512], 128), inc="Sc")
        P.op("scalar", "copy", view(cfm[:, 512:1024], 128),
             view(PS[:, 5 * 512:5 * 512 + 512], 128), inc="Sc")
        P.r_done([4, 5], "Sc")
        P.t_take([0, 1])
        P.wait("tensor", "Sc", P.cnt["Sc"])
        for dc in range(8):
            for half in range(2):
                P.op("tensor", "matmul", PS[:, half * 512:half * 512 + 512],
                     cfmv[:, dc, :], WoBv[:, dc, half * 512:(half + 1) * 512],
                     start=(dc == 0), stop=(dc == 7), inc="T" if half == 1 else None)
        cfm_free = P.cnt["T"]
        P.t_done([0, 1])
        kx, xt, lvx = xs.load(xres[t * 128:(t + 1) * 128, :])
        P.r_wait("vector", [0, 1])
        P.wait("vector", "L", lvx)
        P.wait("vector", "G", ro_free)
        P.op("vector", "scalar_tensor_tensor", ro, PS[:, 0:1024], 1.0, borep,
             ALU.mult, ALU.add)
        P.op("vector", "tensor_add", ro, ro, xt, inc="V")
        P.r_done([0, 1], "V")
        xs.release(kx, "V")
        P.wait("gpsimd", "V", P.cnt["V"])
        ro_free = P.dma("gpsimd", xres[t * 128:(t + 1) * 128, :], ro)
    P.wait("sync", "G", P.cnt["G"])
    ar.pop()
    ar.pop()


def emit_l3_ffn(P, ar, PS, C, IN, SCR, xres, xout):
    ar.push()
    fmA = ar.alloc(8 * 2048)
    fmAv = view(fmA, 2048)
    emit_ln(P, ar, PS, C, xres, dst_fm=fmAv)
    G = ar.alloc(16 * 8)
    Gv = view(G, 8)
    ar.push()
    rT = ar.alloc(8 * 8)
    rTv = view(rT, 8)
    lvr = P.dma("sync", rTv, IN["L3_rT"].rearrange("(c p) q -> p c q", p=128))
    st = ar.alloc(16)
    m1, m2, den = st[:, 8:9], st[:, 9:10], st[:, 10:11]
    lg = ar.alloc(8)
    eq1 = ar.alloc(8)
    eq2 = ar.alloc(8)
    P.wait("tensor", "L", lvr)
    P.wait("tensor", "Sc", P.cnt["Sc"])
    for t in range(NT):
        P.t_take([7])
        for dc in range(8):
            P.op("tensor", "matmul", PS[:, 7 * 512:7 * 512 + 8],
                 fmAv[:, dc, t * 128:(t + 1) * 128], rTv[:, dc, :],
                 start=(dc == 0), stop=(dc == 7), inc="T" if dc == 7 else None)
        P.t_done([7])
        P.r_wait("vector", [7])
        P.op("vector", "reduce_max", m1, PS[:, 7 * 512:7 * 512 + 8], AX.X, negate=True, inc="V")
        P.wait("scalar", "V", P.cnt["V"])
        P.op("scalar", "activation", lg, PS[:, 7 * 512:7 * 512 + 8], AF.Exp,
             bias=m1, scale=1.0, accum_out=den, inc="Sc")
        P.r_done([7], "Sc")
        P.wait("vector", "Sc", P.cnt["Sc"])
        P.op("vector", "reciprocal", den, den)
        P.op("vector", "tensor_scalar", lg, lg, den, None, ALU.mult)
        P.op("vector", "reduce_max", m1, lg, AX.X)
        P.op("vector", "tensor_scalar", eq1, lg, m1, None, ALU.is_equal)
        P.op("vector", "scalar_tensor_tensor", eq2, eq1, -2.0, lg, ALU.mult, ALU.add)
        P.op("vector", "reduce_max", m2, eq2, AX.X)
        P.op("vector", "tensor_scalar", eq2, eq2, m2, None, ALU.is_equal)
        P.op("vector", "tensor_tensor", den, m1, m2, ALU.add)
        P.op("vector", "reciprocal", den, den)
        P.op("vector", "tensor_scalar", eq1, eq1, m1, None, ALU.mult)
        P.op("vector", "tensor_scalar", eq2, eq2, m2, None, ALU.mult)
        P.op("vector", "tensor_add", eq1, eq1, eq2)
        P.op("vector", "tensor_scalar", Gv[:, t, :], eq1, den, None, ALU.mult, inc="V")
    ar.pop()
    ar.push()
    ups = Stream(P, ar, 512, nbuf=3)
    dns = Stream(P, ar, 1024, nbuf=3)
    NG = 4
    he = [ar.alloc(2048) for _ in range(NG)]
    hid = ar.alloc(1024)
    hfm = ar.alloc(8 * 128)
    hfmv = view(hfm, 128)
    acc = [ar.alloc(1024) for _ in range(NG)]
    xs = Stream(P, ar, 1024, nbuf=2)
    ro = ar.alloc(1024)
    ro_free = 0
    hid_free_T = 0
    for g in range(16 // NG):
        tiles = list(range(g * NG, g * NG + NG))
        P.wait("vector", "G", P.cnt["G"])
        for ti in range(NG):
            P.op("vector", "memset", acc[ti], 0.0, inc="V" if ti == NG - 1 else None)
        for e in range(8):
            for ocp in range(4):
                bset = list(range(NG))
                P.t_take(bset)
                P.wait("tensor", "V", P.cnt["V"])
                for dc in range(8):
                    kw, wt, lvw = ups.load(IN["L3_up"][(e * 8 + dc) * 128:
                                                       (e * 8 + dc + 1) * 128,
                                                       ocp * 512:(ocp + 1) * 512])
                    P.wait("tensor", "L", lvw)
                    for ti, t in enumerate(tiles):
                        P.op("tensor", "matmul", PS[:, ti * 512:ti * 512 + 512],
                             fmAv[:, dc, t * 128:(t + 1) * 128], wt,
                             start=(dc == 0), stop=(dc == 7),
                             inc="T" if ti == NG - 1 else None)
                    ups.release(kw, "T")
                P.t_done(bset)
                P.r_wait("scalar", bset)
                for ti in range(NG):
                    P.op("scalar", "copy", he[ti][:, ocp * 512:(ocp + 1) * 512],
                         PS[:, ti * 512:ti * 512 + 512], inc="Sc" if ti == NG - 1 else None)
                P.r_done(bset, "Sc")
            for ti, t in enumerate(tiles):
                P.wait("scalar", "Sc", P.cnt["Sc"])
                P.wait("scalar", "V", P.cnt["V"])
                P.op("scalar", "activation", hid, he[ti][:, 0:1024], AF.Silu, inc="Sc")
                P.wait("vector", "Sc", P.cnt["Sc"])
                P.op("vector", "scalar_tensor_tensor", hid, hid, 1.0,
                     he[ti][:, 1024:2048], ALU.mult, ALU.mult)
                P.op("vector", "tensor_scalar", hid, hid, Gv[:, t, e:e + 1], None,
                     ALU.mult, inc="V")
                P.t_take([6, 7])
                P.wait("tensor", "V", P.cnt["V"])
                P.wait("tensor", "T", hid_free_T)
                for c in range(8):
                    bank = 6 + c // 4
                    P.op("tensor", "transpose",
                         PS[:, bank * 512 + (c % 4) * 128:bank * 512 + (c % 4 + 1) * 128],
                         hid[:, c * 128:(c + 1) * 128], C["ident"],
                         inc="T" if c == 7 else None)
                P.t_done([6, 7])
                P.r_wait("scalar", [6, 7])
                P.op("scalar", "copy", view(hfm, 128)[:, 0:4, :],
                     view(PS[:, 6 * 512:6 * 512 + 512], 128), inc="Sc")
                P.op("scalar", "copy", view(hfm, 128)[:, 4:8, :],
                     view(PS[:, 7 * 512:7 * 512 + 512], 128), inc="Sc")
                P.r_done([6, 7], "Sc")
                P.t_take([4, 5])
                P.wait("tensor", "Sc", P.cnt["Sc"])
                for dc in range(8):
                    kw, wt, lvw = dns.load(IN["L3_down"][(e * 8 + dc) * 128:
                                                         (e * 8 + dc + 1) * 128, :])
                    P.wait("tensor", "L", lvw)
                    for half in range(2):
                        P.op("tensor", "matmul",
                             PS[:, 4 * 512 + half * 512:4 * 512 + half * 512 + 512],
                             hfmv[:, dc, :], wt[:, half * 512:(half + 1) * 512],
                             start=(dc == 0), stop=(dc == 7), inc="T" if half == 1 else None)
                    dns.release(kw, "T")
                hid_free_T = P.cnt["T"]
                P.t_done([4, 5])
                P.r_wait("vector", [4, 5])
                P.op("vector", "tensor_add", acc[ti][:, 0:512], acc[ti][:, 0:512],
                     PS[:, 4 * 512:4 * 512 + 512])
                P.op("vector", "tensor_add", acc[ti][:, 512:1024], acc[ti][:, 512:1024],
                     PS[:, 4 * 512 + 512:4 * 512 + 1024], inc="V")
                P.r_done([4, 5], "V")
        for ti, t in enumerate(tiles):
            kx, xt, lvx = xs.load(xres[t * 128:(t + 1) * 128, :])
            P.wait("vector", "L", lvx)
            P.wait("vector", "G", ro_free)
            P.op("vector", "tensor_add", ro, acc[ti], xt, inc="V")
            xs.release(kx, "V")
            P.wait("gpsimd", "V", P.cnt["V"])
            ro_free = P.dma("gpsimd", xout[t * 128:(t + 1) * 128, :], ro)
    P.wait("sync", "G", P.cnt["G"])
    ar.pop()
    ar.pop()


ARN = 50176

def build_nc(upto=8):
    nc = bass.Bass("TRN2", target_bir_lowering=False)
    IN = {}

    def inp(name, shape):
        IN[name] = nc.dram_tensor(name, list(shape), F32, kind="ExternalInput").ap()
        return IN[name]

    inp("x0", (S, D))
    inp("L0_WqT", (D, D))
    inp("L0_whead", (4, 128, 3072))
    inp("L0_w_up", (3072, 1024))
    inp("L0_w_gate", (3072, 1024))
    inp("L0_w_down", (1024, 3072))
    inp("c_ident", (128, 128))
    inp("c_misc", (128, 8))
    inp("c_ones_col", (128, 1))
    inp("L1_W", (D, D))
    inp("L1_bcol", (128, 8))
    inp("L1_lw", (S, S))
    inp("L1_w1T", (1024, 4096))
    inp("L1_w2T", (4096, 1024))
    inp("L1_b1_rep", (128, 4096))
    inp("L1_b2_rep", (128, 1024))
    inp("L2_Wqkv", (1024, 3072))
    inp("L2_bqkv_col", (128, 16))
    inp("L2_bv_rep", (128, 1024))
    inp("L2_Wout", (1024, 1024))
    inp("L2_bout_rep", (128, 1024))
    inp("L2F_wqkv", (2048, 1536))
    inp("L2F_qs", (2, 128, 64))
    inp("L2F_ks", (2, 128, 64))
    inp("L2F_rope", (4096, 256))
    inp("L2F_mask", (128, 4096))
    inp("L2F_wo", (1024, 1024))
    inp("L2F_lng", (256, 1024))
    inp("L2F_lnb", (256, 1024))
    inp("L2F_w1", (2048, 5460))
    inp("L2F_b1_rep", (256, 5460))
    inp("L2F_w2pad", (5632, 1024))
    inp("L3_WqT", (1024, 1024))
    inp("L3_WkT", (1024, 1024))
    inp("L3_WvT", (1024, 1024))
    inp("L3_WoT", (1024, 1024))
    inp("L3_bqkv_rep", (128, 3072))
    inp("L3_bo_row", (1, 1024))
    inp("L3_bo_rep", (128, 1024))
    inp("L3_rope", (2048, 64))
    inp("L3_dmask", (128, 128))
    inp("L3_rT", (1024, 8))
    inp("L3_up", (8192, 2048))
    inp("L3_down", (8192, 1024))
    inp("c_ones_row", (1, 128))
    inp("c_l0_qnw", (128, D))
    y = nc.dram_tensor("y", [S, D], F32, kind="ExternalOutput").ap()

    SCR = {}
    def scr(name, shape, **kw):
        SCR[name] = nc.dram_tensor("scr_" + name, list(shape), F32, kind="Internal", **kw).ap()
        return SCR[name]
    scr("xres", (S, D))
    scr("wuT", (24576, 128))
    scr("wgT", (24576, 128))
    scr("wdT", (24576, 128))
    scr("s_up", (12288, 512))
    scr("s_gate", (12288, 512))
    scr("s_q", (4096, 512))
    scr("s_x2", (S, D))
    scr("s_x3", (S, D))
    scr("s_qh", (32768, 64))
    scr("s_kh", (32768, 64))
    scr("s_vh", (32768, 64))
    scr("s_v", (S, D))
    scr("wq1", (1024, 4096))
    scr("wq2", (4096, 1024))
    scr("s_h", (S, 4096))
    for i in range(4):
        scr(f"cc_in{i}", (1, 8))
        scr(f"cc_out{i}", (1, 8), addr_space="Shared")

    AR_t = nc.alloc_sbuf_tensor("AR", [128, ARN], F32)
    CONST_t = nc.alloc_sbuf_tensor("CONST", [128, 1536], F32)
    PS_t = nc.alloc_psum_tensor("PS", [128, 4096], F32)
    ARap = AR_t.ap()
    CO = CONST_t.ap()
    PS = PS_t.ap()

    P = Prog(nc)
    ar = Arena(ARap, ARN, P)
    C = {
        "ident": CO[:, 0:128],
        "ones_row": CO[0:1, 128:256],
        "l0_qnw": CO[:, 256:1280],
        "c7": CO[:, 1280:1281],
        "c32767": CO[:, 1281:1282],
        "ones_col": CO[:, 1288:1289],
    }
    P.dma("sync", C["ident"], IN["c_ident"])
    P.dma("sync", C["ones_row"], IN["c_ones_row"])
    P.dma("sync", C["l0_qnw"], IN["c_l0_qnw"])
    P.dma("sync", CO[:, 1280:1288], IN["c_misc"])
    cl = P.dma("sync", CO[:, 1288:1289], IN["c_ones_col"])
    for e in ("vector", "scalar", "tensor", "gpsimd"):
        P.wait(e, "L", cl)

    emit_l0_attn(P, ar, PS, C, IN, IN["x0"], SCR["xres"])
    if upto >= 2:
        emit_l0_ffn(P, ar, PS, C, IN, SCR, SCR["xres"])
    if upto >= 3:
        emit_l1_attn(P, ar, PS, C, IN, SCR, SCR["xres"])
    if upto >= 4:
        emit_l1_ffn(P, ar, PS, C, IN, SCR, SCR["xres"])
    if upto >= 5:
        emit_l2_attn(P, ar, PS, C, IN, SCR, SCR["xres"])
    if upto >= 6:
        emit_l2_ffn(P, ar, PS, C, IN, SCR, SCR["xres"])
    if upto >= 7:
        emit_l3_attn(P, ar, PS, C, IN, SCR, SCR["xres"])
    if upto >= 8:
        emit_l3_ffn(P, ar, PS, C, IN, SCR, SCR["xres"], y)

    P.wait("gpsimd", "V", P.cnt["V"])
    P.wait("gpsimd", "G", P.cnt["G"])
    P.wait("gpsimd", "Sc", P.cnt["Sc"])
    P.wait("gpsimd", "T", P.cnt["T"])
    if upto < 8:
        fv = P.dma("gpsimd", y, SCR["xres"])
        P.wait("gpsimd", "G", fv)
    assert ar.off == 0 or True

    with nc.Block() as block:
        @block.sync
        def _(e):
            P.replay("sync", e)

        @block.scalar
        def _(e):
            P.replay("scalar", e)

        @block.vector
        def _(e):
            P.replay("vector", e)

        @block.tensor
        def _(e):
            P.replay("tensor", e)

        @block.gpsimd
        def _(e):
            P.replay("gpsimd", e)
    return nc


def host_prep(inputs):
    f = np.float32
    ct = lambda a: np.ascontiguousarray(a, dtype=f)
    sh = {}
    qkv_w = inputs["L0_attn_qkv_w"]
    sh["L0_WqT"] = ct(qkv_w[0:1024].T)
    w0T = inputs["L0_attn_w0"].transpose(0, 2, 1)
    w2T = inputs["L0_attn_w2"].transpose(0, 2, 1)
    w1T = inputs["L0_attn_w1"].transpose(0, 2, 1)
    whead = np.zeros((4, 128, 3072), f)
    for h in range(4):
        whead[h, :, 0:1024] = w0T[h].reshape(2, 128, 512).transpose(1, 0, 2).reshape(128, 1024)
        whead[h, :, 1024:2048] = w2T[h].reshape(2, 128, 512).transpose(1, 0, 2).reshape(128, 1024)
        whead[h, :, 2048:3072] = w1T[h].reshape(4, 128, 256).transpose(1, 0, 2).reshape(128, 1024)
    sh["L0_whead"] = ct(whead)
    sh["L0_w_up"] = ct(inputs["L0_ffn_w_up"])
    sh["L0_w_gate"] = ct(inputs["L0_ffn_w_gate"])
    sh["L0_w_down"] = ct(inputs["L0_ffn_w_down"])
    sh["c_ident"] = ct(np.eye(128))
    sh["c_ones_row"] = np.ones((1, 128), f)
    sh["c_l0_qnw"] = ct(np.broadcast_to(inputs["L0_attn_q_norm_w"], (128, D)))
    misc = np.zeros((128, 8), f)
    misc[:, 0] = 7.0
    misc[:, 1] = 32767.0
    sh["c_misc"] = misc
    sh["c_ones_col"] = np.ones((128, 1), f)
    sh["L1_W"] = ct(inputs["L1_attn_W"])
    sh["L1_bcol"] = ct(inputs["L1_attn_b"].reshape(8, 128).T)
    idx = np.arange(S, dtype=np.float64)
    loc = np.abs(idx[None, :] - idx[:, None])
    lw = (1.0 / (loc + 1e-7)).astype(f) * (loc != 0)
    sh["L1_lw"] = ct(lw)
    sh["L1_w1T"] = ct(inputs["L1_ffn_w1"].T)
    sh["L1_w2T"] = ct(inputs["L1_ffn_w2"].T)
    sh["L1_b1_rep"] = ct(np.broadcast_to(inputs["L1_ffn_b1"], (128, 4096)))
    sh["L1_b2_rep"] = ct(np.broadcast_to(inputs["L1_ffn_b2"], (128, 1024)))
    sh["L2_Wqkv"] = ct(inputs["L2_attn_W_qkv"])
    sh["L2_bqkv_col"] = ct(inputs["L2_attn_b_qkv"][0:2048].reshape(16, 128).T)
    sh["L2_bv_rep"] = ct(np.broadcast_to(inputs["L2_attn_b_qkv"][2048:3072], (128, 1024)))
    sh["L2_Wout"] = ct(inputs["L2_attn_W_out"])
    sh["L2_bout_rep"] = ct(np.broadcast_to(inputs["L2_attn_b_out"], (128, D)))
    sh["L2F_wqkv"] = ct(inputs["L2_ffn_wqkv"].reshape(2048, 1536))
    sh["L2F_qs"] = ct(np.broadcast_to(inputs["L2_ffn_q_scale"][:, None, :], (2, 128, 64)))
    sh["L2F_ks"] = ct(np.broadcast_to(inputs["L2_ffn_k_scale"][:, None, :], (2, 128, 64)))
    sh["L2F_wo"] = ct(inputs["L2_ffn_wo"].reshape(1024, 1024))
    sh["L2F_lng"] = ct(np.repeat(inputs["L2_ffn_ln_g"][:, None, :], 128, axis=1).reshape(256, 1024))
    sh["L2F_lnb"] = ct(np.repeat(inputs["L2_ffn_ln_b"][:, None, :], 128, axis=1).reshape(256, 1024))
    sh["L2F_w1"] = ct(inputs["L2_ffn_w1"].reshape(2048, 5460))
    sh["L2F_b1_rep"] = ct(np.repeat(inputs["L2_ffn_b1"][:, None, :], 128, axis=1).reshape(256, 5460))
    w2p = np.zeros((2, 2816, 1024), f)
    w2p[:, 0:2730, :] = inputs["L2_ffn_w2"]
    w2p[:, 2730, :] = inputs["L2_ffn_b2"]
    sh["L2F_w2pad"] = ct(w2p.reshape(5632, 1024))
    DH2 = 64
    half_idx = np.arange(0, DH2, 2, dtype=np.float64)
    inv_freq = 1.0 / 10000.0 ** (half_idx / DH2)
    tpos = np.arange(S, dtype=np.float64)
    freqs = np.concatenate([tpos[:, None] * inv_freq[None, :]] * 2, axis=-1)
    base = (half_idx + 0.4 * DH2) / (1.4 * DH2)
    power = (tpos - S // 2) / 256.0
    xsc = np.concatenate([base[None, :] ** power[:, None]] * 2, axis=-1)
    cosv, sinv = np.cos(freqs), np.sin(freqs)
    def signflip(a):
        b = a.copy(); b[:, 0:32] = -b[:, 0:32]; return b
    rope = np.zeros((2, S, 256), np.float64)
    for l in range(2):
        rope[l, :, 0:64] = cosv * xsc * 8.0
        rope[l, :, 64:128] = signflip(sinv * xsc) * 8.0
        rope[l, :, 128:192] = cosv / xsc
        rope[l, :, 192:256] = signflip(sinv / xsc)
    sh["L2F_rope"] = ct(rope.reshape(4096, 256))
    m = np.full((512, 1024), NEG, f)
    il = np.arange(512)[:, None]
    bb = np.arange(512)[None, :]
    m[:, 0:512] = np.where(bb >= il, 0.0, NEG)
    m[:, 512:1024] = np.where(bb <= il, 0.0, NEG)
    sh["L2F_mask"] = ct(m.reshape(4, 128, 1024).transpose(1, 0, 2).reshape(128, 4096))
    sh["L3_WqT"] = ct(inputs["L3_attn_WQ_w"].T)
    sh["L3_WkT"] = ct(inputs["L3_attn_WK_w"].T)
    sh["L3_WvT"] = ct(inputs["L3_attn_WV_w"].T)
    sh["L3_WoT"] = ct(inputs["L3_attn_Wo_w"].T)
    sh["L3_bqkv_rep"] = ct(np.broadcast_to(np.concatenate(
        [inputs["L3_attn_WQ_b"], inputs["L3_attn_WK_b"], inputs["L3_attn_WV_b"]]),
        (128, 3072)))
    sh["L3_bo_row"] = ct(inputs["L3_attn_Wo_b"].reshape(1, -1))
    sh["L3_bo_rep"] = ct(np.broadcast_to(inputs["L3_attn_Wo_b"], (128, 1024)))
    dk = 64
    fr3 = 1.0 / 10000.0 ** (2.0 * np.arange(dk // 2, dtype=np.float64) / dk)
    ang = np.arange(S, dtype=np.float64)[:, None] * fr3[None, :]
    rp3 = np.concatenate([np.cos(ang), np.sin(ang)], axis=-1)
    sh["L3_rope"] = ct(rp3)
    ii = np.arange(128)
    dm = np.where(ii[None, :] <= ii[:, None], 0.0, NEG).astype(f)
    sh["L3_dmask"] = ct(dm)
    sh["L3_rT"] = ct(inputs["L3_ffn_router_w"].T)
    sh["L3_up"] = ct(inputs["L3_ffn_up_proj"].reshape(8192, 2048))
    sh["L3_down"] = ct(inputs["L3_ffn_down_proj"].reshape(8192, 1024))
    xs = np.asarray(inputs["L0_attn_hidden_states"], dtype=f)
    cores = [dict(sh, x0=ct(xs[i])) for i in range(8)]
    return cores


def kernel(**inputs):
    cores = host_prep(inputs)
    nc = build_nc(upto=8)
    res = run_bass_kernel_spmd(nc, cores, core_ids=list(range(8)))
    return np.stack([res.results[i]["y"] for i in range(8)]).astype(np.float32)
